# version 02 (final)
# baseline (speedup 1.0000x reference)
import contextlib
import numpy as np
import ml_dtypes
import concourse.bass as bass
import concourse.mybir as mybir
from concourse.bass_utils import run_bass_kernel_spmd
from concourse.bass_core_rust import IndirectOffsetOnAxis

F32 = mybir.dt.float32
BF16 = mybir.dt.bfloat16
I32 = mybir.dt.int32
AF = mybir.ActivationFunctionType
ALU = mybir.AluOpType
AX = mybir.AxisListType

NCORES = 8
NO_CC = False
DBG_FFN = False
L2_ATTN_ONLY = False
L2_STAGE = 99
L3_NO_RELV = False
MCT = 99
SKIPEXP = False
T = 2048
D = 1024
NT = T // 128
DC = D // 128
MAGIC = 12582912.0
LN_EPS = 1e-5

ENGINES = ["pe", "ve", "se", "gp", "sy"]
ENG_ATTR = {"pe": "tensor", "ve": "vector", "se": "scalar", "gp": "gpsimd", "sy": "sync"}


class Res:
    __slots__ = ("name", "last_w", "readers")

    def __init__(self, name="", last_w=None):
        self.name = name
        self.last_w = last_w
        self.readers = []


class _Step:
    __slots__ = ("idx", "engine", "fn", "deps", "weight", "end_cnt", "tag")

    def __init__(self, idx, engine, fn, deps, weight, tag):
        self.idx, self.engine, self.fn = idx, engine, fn
        self.deps, self.weight, self.tag = deps, weight, tag
        self.end_cnt = None


class Ctx:
    def __init__(self, nc):
        self.nc = nc
        self.steps = []
        self.stack = contextlib.ExitStack()
        self.res = {}
        self.cur_fence = None
        self._names = set()

    def sb(self, name, shape, dt=F32, align=None):
        assert name not in self._names, name
        self._names.add(name)
        kw = {} if align is None else {"align_bytes": align}
        t = self.stack.enter_context(self.nc.sbuf_tensor(name, shape, dt, **kw))
        self.res[name] = Res(name, self.cur_fence)
        return t

    def psum(self, name, shape, dt=F32):
        assert name not in self._names, name
        self._names.add(name)
        t = self.stack.enter_context(self.nc.psum_tensor(name, shape, dt))
        self.res[name] = Res(name, self.cur_fence)
        return t

    def newres(self, name):
        self.res[name] = Res(name, self.cur_fence)
        return self.res[name]

    def r(self, *names):
        return [self.res[n] for n in names]

    def step(self, engine, fn, reads=(), writes=(), deps=(), weight=1, tag=""):
        reads = [self.res[x] if isinstance(x, str) else x for x in reads]
        writes = [self.res[x] if isinstance(x, str) else x for x in writes]
        dset = set(d for d in deps if d is not None)
        for r in reads:
            if r.last_w is not None:
                dset.add(r.last_w)
        for w in writes:
            if w.last_w is not None:
                dset.add(w.last_w)
            dset.update(w.readers)
        s = _Step(len(self.steps), engine, fn, sorted(dset), weight, tag)
        self.steps.append(s)
        for r in reads:
            r.readers.append(s.idx)
        for w in writes:
            w.last_w = s.idx
            w.readers = []
        return s.idx

    def fence(self, tag="fence"):
        last = {}
        for s in self.steps:
            last[s.engine] = s.idx
        tok = self.step("ve", lambda e, inc, mk: None,
                        deps=list(last.values()), tag=tag)
        self.cur_fence = tok
        return tok

    def finish(self):
        cnt = {e: 0 for e in ENGINES}
        for s in self.steps:
            cnt[s.engine] += s.weight
            s.end_cnt = cnt[s.engine]
        sem_cms, sems = [], {}
        for e in ENGINES:
            cm = self.nc.semaphore("sem_" + e)
            sems[e] = cm.__enter__()
            sem_cms.append(cm)
        with self.nc.Block() as block:
            for eng_name in ENGINES:
                mysteps = [s for s in self.steps if s.engine == eng_name]
                if not mysteps:
                    continue

                def make_body(mysteps=mysteps, eng_name=eng_name):
                    def body(eng):
                        watermark = {e: 0 for e in ENGINES}
                        for s in mysteps:
                            need = {}
                            for d in s.deps:
                                ds = self.steps[d]
                                if (ds.engine == eng_name and eng_name == "pe"
                                        and ds.weight < 16):
                                    continue
                                need[ds.engine] = max(need.get(ds.engine, 0), ds.end_cnt)
                            for e2 in ENGINES:
                                cx = need.get(e2, 0)
                                if cx > watermark[e2]:
                                    watermark[e2] = cx
                                    eng.wait_ge(sems[e2], cx)
                            state = {"total": 0, "last": None}

                            def inc(instr, k=16):
                                instr.then_inc(sems[eng_name], k)
                                state["total"] += k
                                return instr

                            def mk(instr):
                                state["last"] = instr
                                return instr

                            s.fn(eng, inc, mk)
                            rem = s.weight - state["total"]
                            assert rem >= 0, f"step {s.idx} {s.tag}"
                            if rem > 0:
                                if state["last"] is not None:
                                    state["last"].then_inc(sems[eng_name], rem)
                                else:
                                    eng.nop().then_inc(sems[eng_name], rem)
                    return body

                getattr(block, ENG_ATTR[eng_name])(make_body())
        self.stack.close()
        for cm in reversed(sem_cms):
            cm.__exit__(None, None, None)


def host_constants():
    return {
        "ones128": np.ones((1, 128), np.float32),
        "identb": np.eye(128).astype(ml_dtypes.bfloat16),
        "identf": np.eye(128).astype(np.float32),
        "triT": np.triu(np.ones((128, 128), np.float32)).astype(ml_dtypes.bfloat16),
        "upper_strict": np.triu(np.ones((128, 128), np.float32), 1),
    }


CONST_SPECS = {
    "ones128": ([1, 128], F32),
    "identb": ([128, 128], BF16),
    "identf": ([128, 128], F32),
    "triT": ([128, 128], BF16),
    "upper_strict": ([128, 128], F32),
}


def declare_common(c):
    nc = c.nc
    c.cst = {}
    for k, (shape, dt) in CONST_SPECS.items():
        dr = nc.dram_tensor("cst_" + k, shape, dt, kind="ExternalInput").ap()
        t = c.sb("c_" + k, shape, dt)
        c.cst[k] = t
        c.step("sy", lambda e, inc, mk, dr=dr, t=t: inc(e.dma_start(out=t[:, :], in_=dr[:, :])),
               writes=["c_" + k], weight=16, tag="cst")
    c.t_xt = c.sb("xt", [128, D])
    c.t_sq = c.sb("sq", [128, D])
    c.t_sqb = c.sb("sqb", [128, D], BF16)
    c.t_xn32 = c.sb("xn32", [128, D])
    c.t_mu = c.sb("stat_mu", [128, NT])
    c.t_rs = c.sb("stat_rs", [128, NT])
    c.t_red = c.sb("redrow", [128, 2])
    c.t_row1 = c.sb("row1", [1, 128])
    c.t_scal = c.sb("scal", [1, 16])
    c.t_repA = c.sb("repA", [128, 8])
    c.t_tiny = c.sb("tiny", [1, 8])
    c.t_onesf = c.sb("onesf", [128, 1])
    c.step("ve", lambda e, inc, mk: mk(e.memset(c.t_onesf[:, :], 1.0)),
           writes=["onesf"], tag="onesf")
    c.t_psA = c.psum("psA", [128, 2048])
    c.t_psU = c.psum("psU", [128, 2048])


def emit_ln_pass(c, x_in, xnfm, xnfm_res):
    xt, sq, sqb, xn32 = c.t_xt, c.t_sq, c.t_sqb, c.t_xn32
    mu, rs, red, psA = c.t_mu, c.t_rs, c.t_red, c.t_psA
    idf = c.cst["identf"]

    c.step("ve", lambda e, inc, mk: mk(e.memset(red[:, 0:1], 0.0)),
           writes=["redrow"], tag="ln.i")
    for t in range(NT):
        c.step("sy", lambda e, inc, mk, t=t: inc(e.dma_start(
            out=xt[:, :], in_=x_in[t * 128:(t + 1) * 128, :])),
            writes=["xt"], weight=16, tag=f"ln.l{t}")
        c.step("ve", lambda e, inc, mk, t=t: mk(e.reduce_sum(
            mu[:, t:t + 1], xt[:, :], AX.X)),
            reads=["xt"], writes=["stat_mu"], tag="ln.s")
        c.step("se", lambda e, inc, mk: mk(e.activation(
            out=sq[:, :], in_=xt[:, :], func=AF.Square)),
            reads=["xt"], writes=["sq"], tag="ln.q")

        c.step("ve", lambda e, inc, mk, t=t: mk(e.reduce_sum(
            rs[:, t:t + 1], sq[:, :], AX.X)),
            reads=["sq"], writes=["stat_rs"], tag="ln.v1")
        c.step("ve", lambda e, inc, mk, t=t: mk(e.tensor_scalar(
            sq[:, 0:1], mu[:, t:t + 1], mu[:, t:t + 1], -1.0 / (D * D),
            ALU.mult, ALU.mult)),
            reads=["stat_mu"], writes=["sq"], tag="ln.v2")
        c.step("ve", lambda e, inc, mk, t=t: mk(e.tensor_scalar(
            mu[:, t:t + 1], mu[:, t:t + 1], 1.0 / D, None, ALU.mult)),
            writes=["stat_mu"], tag="ln.v3")
        c.step("ve", lambda e, inc, mk, t=t: mk(e.scalar_tensor_tensor(
            rs[:, t:t + 1], rs[:, t:t + 1], 1.0 / D, sq[:, 0:1], ALU.mult, ALU.add)),
            reads=["sq"], writes=["stat_rs"], tag="ln.v4")
        c.step("ve", lambda e, inc, mk, t=t: mk(e.tensor_scalar(
            rs[:, t:t + 1], rs[:, t:t + 1], LN_EPS, None, ALU.add)),
            writes=["stat_rs"], tag="ln.eps")
        c.step("se", lambda e, inc, mk, t=t: mk(e.activation(
            out=rs[:, t:t + 1], in_=rs[:, t:t + 1], func=AF.Sqrt)),
            writes=["stat_rs"], tag="ln.sq")

        c.step("ve", lambda e, inc, mk, t=t: mk(e.reciprocal(
            rs[:, t:t + 1], rs[:, t:t + 1])), writes=["stat_rs"], tag="ln.n1")
        c.step("ve", lambda e, inc, mk, t=t: mk(e.tensor_scalar(
            xn32[:, :], xt[:, :], mu[:, t:t + 1], rs[:, t:t + 1],
            ALU.subtract, ALU.mult)),
            reads=["xt", "stat_mu", "stat_rs"], writes=["xn32"], tag="ln.n2")
        c.step("ve", lambda e, inc, mk, t=t: mk(e.reduce_max(
            red[:, 1:2], xn32[:, :], AX.X, apply_absolute_value=True)),
            reads=["xn32"], writes=["redrow"], tag="ln.n3")
        c.step("ve", lambda e, inc, mk, t=t: mk(e.tensor_max(
            red[:, 0:1], red[:, 0:1], red[:, 1:2])),
            writes=["redrow"], tag="ln.n4")

        def ptr(e, inc, mk, t=t):
            for ch in range(DC):
                mk(e.transpose(c.t_psA[:, ch * 128:(ch + 1) * 128],
                               xn32[:, ch * 128:(ch + 1) * 128], idf[:, :]))
        c.step("pe", ptr, reads=["xn32", "c_identf"], writes=["psA"], tag="ln.t")

        def vst(e, inc, mk, t=t):
            for ch in range(DC):
                ins = e.tensor_copy(xnfm[:, ch * T + t * 128:ch * T + (t + 1) * 128],
                                    c.t_psA[:, ch * 128:(ch + 1) * 128])
            mk(ins)
        c.step("ve", vst, reads=["psA"], writes=[xnfm_res], tag="ln.c")


def emit_allreduce_max(c, name, scol):
    cc_in = c.nc.dram_tensor(name + "_i", [1, 128], F32, kind="Internal")
    cc_out = c.nc.dram_tensor(name + "_o", [1, 128], F32, kind="Internal")
    cc_res_in = c.newres(name + "_i")
    cc_res_out = c.newres(name + "_o")
    row = 0
    """redrow[:,0] -> AllReduce(max) across cores -> scal[scal_rc]."""
    red, row1, psA, scal = c.t_red, c.t_row1, c.t_psA, c.t_scal
    c.step("pe", lambda e, inc, mk: mk(e.transpose(
        psA[0:1, 0:128], red[:, 0:1], c.cst["identf"][:, :])),
        reads=["redrow", "c_identf"], writes=["psA"], tag="ar.t")
    c.step("ve", lambda e, inc, mk: mk(e.tensor_copy(row1[0:1, :], psA[0:1, 0:128])),
           reads=["psA"], writes=["row1"], tag="ar.c")
    c.step("gp", lambda e, inc, mk: inc(e.dma_start(
        out=cc_in[row:row + 1, :], in_=row1[0:1, :])),
        reads=["row1"], writes=[cc_res_in], weight=16, tag="ar.o")
    if NO_CC:
        c.step("gp", lambda e, inc, mk: inc(e.dma_start(
            out=cc_out.ap()[row:row + 1, :], in_=cc_in.ap()[row:row + 1, :])),
            reads=[cc_res_in], writes=[cc_res_out], weight=16, tag="ar.cc0")
    else:
        c.step("gp", lambda e, inc, mk: inc(e.collective_compute(
            "AllReduce", ALU.max, replica_groups=[list(range(NCORES))],
            ins=[cc_in.ap()[row:row + 1, :].opt()],
            outs=[cc_out.ap()[row:row + 1, :].opt()]), 1),
            reads=[cc_res_in], writes=[cc_res_out], weight=1, tag="ar.cc")
    c.step("gp", lambda e, inc, mk: inc(e.dma_start(
        out=row1[0:1, :], in_=cc_out[row:row + 1, :])),
        reads=[cc_res_out], writes=["row1"], weight=16, tag="ar.b")
    c.step("ve", lambda e, inc, mk: mk(e.reduce_max(
        c.t_scal[0:1, scol:scol + 1], row1[0:1, :], AX.X)),
        reads=["row1"], writes=["scal"], tag="ar.r")


def emit_replicate(c, scol, rep_col):
    c.step("pe", lambda e, inc, mk: mk(e.matmul(
        c.t_psA[:, 0:1], c.cst["ones128"][0:1, :],
        c.t_scal[0:1, scol:scol + 1], start=True, stop=True)),
        reads=["scal", "c_ones128"], writes=["psA"], tag="rep.m")
    c.step("ve", lambda e, inc, mk: mk(e.tensor_copy(
        c.t_repA[:, rep_col:rep_col + 1], c.t_psA[:, 0:1])),
        reads=["psA"], writes=["repA"], tag="rep.c")


def emit_quant_scale(c, tiny_col, gcol, icol):
    c.step("ve", lambda e, inc, mk: mk(e.tensor_scalar(
        c.t_scal[0:1, gcol:gcol + 1], c.t_scal[0:1, gcol:gcol + 1],
        c.t_tiny[0:1, tiny_col:tiny_col + 1], 1.0 / 127.0, ALU.mult, ALU.mult)),
        reads=["tiny"], writes=["scal"], tag="qs1")
    c.step("ve", lambda e, inc, mk: mk(e.tensor_copy(
        c.t_row1[0:1, :],
        bass.AP(tensor=c.t_scal, offset=gcol, ap=[[16, 1], [0, 128]]))),
        reads=["scal"], writes=["row1"], tag="qs2")
    c.step("ve", lambda e, inc, mk: mk(e.reciprocal(
        c.t_row1[0:1, :], c.t_row1[0:1, :])), writes=["row1"], tag="qs3")
    c.step("ve", lambda e, inc, mk: mk(e.tensor_copy(
        c.t_scal[0:1, icol:icol + 1], c.t_row1[0:1, 0:1])),
        reads=["row1"], writes=["scal"], tag="qs4")


def emit_scale_floor(c, gcol, icol, minv):
    c.step("ve", lambda e, inc, mk: mk(e.tensor_scalar(
        c.t_scal[0:1, gcol:gcol + 1], c.t_scal[0:1, gcol:gcol + 1],
        minv, 1.0 / 127.0, ALU.max, ALU.mult)),
        writes=["scal"], tag="sf1")
    c.step("ve", lambda e, inc, mk: mk(e.tensor_copy(
        c.t_row1[0:1, :],
        bass.AP(tensor=c.t_scal, offset=gcol, ap=[[16, 1], [0, 128]]))),
        reads=["scal"], writes=["row1"], tag="sf2")
    c.step("ve", lambda e, inc, mk: mk(e.reciprocal(
        c.t_row1[0:1, :], c.t_row1[0:1, :])), writes=["row1"], tag="sf3")
    c.step("ve", lambda e, inc, mk: mk(e.tensor_copy(
        c.t_scal[0:1, icol:icol + 1], c.t_row1[0:1, 0:1])),
        reads=["row1"], writes=["scal"], tag="sf4")


def emit_local_absmax(c, name, scol):
    c.step("pe", lambda e, inc, mk: mk(e.transpose(
        c.t_psA[0:1, 0:128], c.t_red[:, 0:1], c.cst["identf"][:, :])),
        reads=["redrow", "c_identf"], writes=["psA"], tag="lm.t")
    c.step("ve", lambda e, inc, mk: mk(e.tensor_copy(
        c.t_row1[0:1, :], c.t_psA[0:1, 0:128])),
        reads=["psA"], writes=["row1"], tag="lm.c")
    c.step("ve", lambda e, inc, mk: mk(e.reduce_max(
        c.t_scal[0:1, scol:scol + 1], c.t_row1[0:1, :], AX.X)),
        reads=["row1"], writes=["scal"], tag="lm.r")


def emit_quant_fm(c, x_in, xqfm, xqfm_res, inv_col):
    xt, sq, sqb, xn32, psA = c.t_xt, c.t_sq, c.t_sqb, c.t_xn32, c.t_psA
    mu, rs, repA = c.t_mu, c.t_rs, c.t_repA
    idf = c.cst["identf"]
    for t in range(NT):
        c.step("sy", lambda e, inc, mk, t=t: inc(e.dma_start(
            out=xt[:, :], in_=x_in[t * 128:(t + 1) * 128, :])),
            writes=["xt"], weight=16, tag=f"qz.l{t}")
        c.step("ve", lambda e, inc, mk, t=t: mk(e.tensor_scalar(
            xn32[:, :], xt[:, :], mu[:, t:t + 1], rs[:, t:t + 1],
            ALU.subtract, ALU.mult)),
            reads=["xt", "stat_mu", "stat_rs"], writes=["xn32"], tag="qz.n")
        c.step("se", lambda e, inc, mk: mk(e.activation(
            out=sq[:, :], in_=xn32[:, :], func=AF.Identity, scale=repA[:, inv_col:inv_col + 1])),
            reads=["xn32", "repA"], writes=["sq"], tag="qz.s")

        c.step("ve", lambda e, inc, mk: mk(e.tensor_scalar(
            sq[:, :], sq[:, :], MAGIC, MAGIC, ALU.add, ALU.subtract)),
            writes=["sq"], tag="qz.r1")
        c.step("ve", lambda e, inc, mk: mk(e.tensor_scalar(
            sq[:, :], sq[:, :], -128.0, 127.0, ALU.max, ALU.min)),
            writes=["sq"], tag="qz.r2")

        def ptr(e, inc, mk, t=t):
            for ch in range(DC):
                mk(e.transpose(psA[:, ch * 128:(ch + 1) * 128],
                               sq[:, ch * 128:(ch + 1) * 128], idf[:, :]))
        c.step("pe", ptr, reads=["sq", "c_identf"], writes=["psA"], tag="qz.t")

        def vst(e, inc, mk, t=t):
            for ch in range(DC):
                ins = e.tensor_copy(xqfm[:, ch * T + t * 128:ch * T + (t + 1) * 128],
                                    psA[:, ch * 128:(ch + 1) * 128])
            mk(ins)
        c.step("ve", vst, reads=["psA"], writes=[xqfm_res], tag="qz.c")


def emit_bias_cols(c, vec_dram, n, dst, dst_res, col0=0, src_off=0):
    xt = c.t_xt
    for r in range((n + 1023) // 1024):
        w = min(1024, n - r * 1024)
        nch = w // 128
        c.step("sy", lambda e, inc, mk, r=r, w=w: inc(e.dma_start(
            out=xt[0:1, 0:w],
            in_=vec_dram[None, src_off + r * 1024:src_off + r * 1024 + w])),
            writes=["xt"], weight=16, tag="bc.l")

        def ptr(e, inc, mk, nch=nch):
            for cc in range(nch):
                mk(e.transpose(c.t_psA[0:128, cc:cc + 1],
                               xt[0:1, cc * 128:(cc + 1) * 128],
                               c.cst["identf"][0:1, 0:1]))
        c.step("pe", ptr, reads=["xt", "c_identf"], writes=["psA"], tag="bc.t")
        c.step("ve", lambda e, inc, mk, r=r, nch=nch: mk(e.tensor_copy(
            dst[:, col0 + r * 8:col0 + r * 8 + nch], c.t_psA[:, 0:nch])),
            reads=["psA"], writes=[dst_res], tag="bc.c")


def emit_residual_combine(c, x_in, x_out, x_out_res, pacc, pacc_res="f1pacc"):
    for t in range(NT):
        c.step("sy", lambda e, inc, mk, t=t: inc(e.dma_start(
            out=c.t_xt[:, :], in_=x_in[t * 128:(t + 1) * 128, :])),
            writes=["xt"], weight=16, tag=f"rc.l{t}")
        c.step("ve", lambda e, inc, mk, t=t: mk(e.tensor_add(
            c.t_xt[:, :], c.t_xt[:, :], pacc[:, t * D:(t + 1) * D])),
            reads=[pacc_res], writes=["xt"], tag="rc.a")
        c.step("gp", lambda e, inc, mk, t=t: inc(e.dma_start(
            out=x_out[t * 128:(t + 1) * 128, :], in_=c.t_xt[:, :])),
            reads=["xt"], writes=[x_out_res], weight=16, tag=f"rc.s{t}")


class scoped:
    def __init__(self, c):
        self.c = c

    def __enter__(self):
        self.outer = self.c.stack
        self.c.stack = contextlib.ExitStack()
        return self

    def __exit__(self, *a):
        self.c.stack.close()
        self.c.stack = self.outer
        self.c.fence()
        return False


def emit_swiglu_ffn(c, pfx, W, x_in, s_fm, s_fm_res):
    nc = c.nc
    I = 2816
    NI = I // 128
    psA, psU = c.t_psA, c.t_psU
    scal, repA, red = c.t_scal, c.t_repA, c.t_red

    kind = "ExternalOutput" if DBG_FFN else "Internal"
    hbuf = nc.dram_tensor(pfx + "_hbuf", [I, T], F32, kind=kind)
    wgq_d = nc.dram_tensor(pfx + "_wgq", [D, I], BF16, kind=kind)
    wuq_d = nc.dram_tensor(pfx + "_wuq", [D, I], BF16, kind="Internal")
    wdq_d = nc.dram_tensor(pfx + "_wdq", [I, D], BF16, kind="Internal")
    r_hbuf = c.newres(pfx + "hbuf")
    r_wgq, r_wuq, r_wdq = (c.newres(pfx + k) for k in ("wgq", "wuq", "wdq"))

    scli = c.sb(pfx + "scli", [128, 2 * NI + DC])
    xqfm = c.sb(pfx + "xqfm", [128, DC * T], BF16)
    pacc = c.sb(pfx + "pacc", [128, NT * D])
    scpp = c.sb(pfx + "scpp", [128, 4])
    c.newres(pfx + "s_use")

    emit_quant_fm(c, x_in, xqfm, pfx + "xqfm", 0)

    with scoped(c):
        wnat = c.sb(pfx + "wnat", [128, I])
        invr = c.sb(pfx + "invr", [1, I])
        invb = c.sb(pfx + "invb", [128, I])
        wq16 = c.sb(pfx + "wq16", [128, I], BF16)
        srow = c.sb(pfx + "srow", [32, 128])

        for wkey, col0, nch, width in (("w_gate", 0, NI, D), ("w_up", NI, NI, D),
                                       ("w_down", 2 * NI, DC, I)):
            for icx in range(nch):
                c.step("sy", lambda e, inc, mk, wkey=wkey, icx=icx, width=width: inc(
                    e.dma_start(out=wnat[:, 0:width],
                                in_=W[wkey][icx * 128:(icx + 1) * 128, :])),
                    writes=[pfx + "wnat"], weight=16, tag=f"{pfx}wm.l")
                c.step("ve", lambda e, inc, mk, icx=icx, col0=col0, width=width: mk(
                    e.reduce_max(scli[:, col0 + icx:col0 + icx + 1], wnat[:, 0:width],
                                 AX.X, apply_absolute_value=True)),
                    reads=[pfx + "wnat"], writes=[pfx + "scli"], tag=f"{pfx}wm.r")
        c.step("ve", lambda e, inc, mk: mk(e.tensor_scalar(
            scli[:, :], scli[:, :], 1.0 / 127.0, None, ALU.mult)),
            writes=[pfx + "scli"], tag=f"{pfx}wm.s")

        def qwt(wkey_t, dst_d, dst_r, nrch, width, col0):
            nw = width // 128
            c.step("pe", lambda e, inc, mk: mk(e.transpose(
                psA[0:nw, 0:128], scli[:, col0:col0 + nw], c.cst["identf"][:, :])),
                reads=[pfx + "scli", "c_identf"], writes=["psA"], tag=f"{pfx}qw.t")

            c.step("ve", lambda e, inc, mk, nw=nw: mk(e.tensor_copy(
                srow[0:nw, :], psA[0:nw, 0:128])),
                reads=["psA"], writes=[pfx + "srow"], tag=f"{pfx}qw.r1")
            c.step("sy", lambda e, inc, mk, nw=nw, width=width: inc(e.dma_start(
                out=bass.AP(tensor=invr, offset=0, ap=[[width, 1], [128, nw], [1, 128]]),
                in_=srow[0:nw, :])),
                reads=[pfx + "srow"], writes=[pfx + "invr"], weight=16, tag=f"{pfx}qw.r2")
            c.step("ve", lambda e, inc, mk: mk(e.reciprocal(
                invr[0:1, 0:width], invr[0:1, 0:width])),
                writes=[pfx + "invr"], tag=f"{pfx}qw.i")

            for seg in range(0, width, 2048):
                segw = min(2048, width - seg)

                def pbc(e, inc, mk, seg=seg, segw=segw):
                    for h in range(0, segw, 512):
                        n = min(512, segw - h)
                        mk(e.matmul(psU[:, h:h + n], c.cst["ones128"][0:1, :],
                                    invr[0:1, seg + h:seg + h + n],
                                    start=True, stop=True))
                c.step("pe", pbc, reads=[pfx + "invr", "c_ones128"], writes=["psU"],
                       tag=f"{pfx}qw.b")
                c.step("ve", lambda e, inc, mk, seg=seg, segw=segw: mk(e.tensor_copy(
                    invb[:, seg:seg + segw], psU[:, 0:segw])),
                    reads=["psU"], writes=[pfx + "invb"], tag=f"{pfx}qw.c")

            for rc in range(nrch):
                c.step("sy", lambda e, inc, mk, rc=rc: inc(e.dma_start(
                    out=wnat[:, 0:width], in_=W[wkey_t][rc * 128:(rc + 1) * 128, :])),
                    writes=[pfx + "wnat"], weight=16, tag=f"{pfx}qw.l{rc}")

                c.step("ve", lambda e, inc, mk, width=width: mk(e.tensor_mul(
                    wnat[:, 0:width], wnat[:, 0:width], invb[:, 0:width])),
                    reads=[pfx + "invb"], writes=[pfx + "wnat"], tag=f"{pfx}qw.q1")
                c.step("ve", lambda e, inc, mk, width=width: mk(e.tensor_scalar(
                    wnat[:, 0:width], wnat[:, 0:width], MAGIC, MAGIC,
                    ALU.add, ALU.subtract)),
                    writes=[pfx + "wnat"], tag=f"{pfx}qw.q2")
                c.step("ve", lambda e, inc, mk, width=width: mk(e.tensor_scalar(
                    wq16[:, 0:width], wnat[:, 0:width], -128.0, 127.0,
                    ALU.max, ALU.min)),
                    reads=[pfx + "wnat"], writes=[pfx + "wq16"], tag=f"{pfx}qw.q3")
                c.step("gp", lambda e, inc, mk, rc=rc, dst_d=dst_d, width=width: inc(
                    e.dma_start(out=dst_d[rc * 128:(rc + 1) * 128, :],
                                in_=wq16[:, 0:width])),
                    reads=[pfx + "wq16"], writes=[dst_r], weight=16, tag=f"{pfx}qw.s{rc}")

        qwt("w_gate_t", wgq_d, r_wgq, DC, I, 0)
        qwt("w_up_t", wuq_d, r_wuq, DC, I, NI)
        qwt("w_down_t", wdq_d, r_wdq, NI, D, 2 * NI)

    with scoped(c):
        wsl = c.sb(pfx + "wsl", [128, 2 * D], BF16)
        gate = c.sb(pfx + "gate", [128, T])
        hsb = c.sb(pfx + "hsb", [128, T])

        c.step("ve", lambda e, inc, mk: mk(e.memset(red[:, 0:1], 0.0)),
               writes=["redrow"], tag=f"{pfx}h.i")
        for ic in range(NI):
            def syl(e, inc, mk, ic=ic):
                inc(e.dma_start(
                    out=wsl[:, 0:D].rearrange("p (c j) -> p c j", j=128),
                    in_=bass.AP(tensor=wgq_d, offset=ic * 128,
                                ap=[[I, 128], [128 * I, DC], [1, 128]])))
                inc(e.dma_start(
                    out=wsl[:, D:2 * D].rearrange("p (c j) -> p c j", j=128),
                    in_=bass.AP(tensor=wuq_d, offset=ic * 128,
                                ap=[[I, 128], [128 * I, DC], [1, 128]])))
            c.step("sy", syl, reads=[r_wgq, r_wuq], writes=[pfx + "wsl"],
                   weight=32, tag=f"{pfx}h.w{ic}")

            def pgu(e, inc, mk, ic=ic):
                for half in range(2):
                    dst = psA if half == 0 else psU
                    woff = half * D
                    for n4 in range(4):
                        for ch in range(DC):
                            mk(e.matmul(dst[:, n4 * 512:(n4 + 1) * 512],
                                        wsl[:, woff + ch * 128:woff + (ch + 1) * 128],
                                        xqfm[:, ch * T + n4 * 512:ch * T + (n4 + 1) * 512],
                                        start=(ch == 0), stop=(ch == DC - 1)))
            c.step("pe", pgu, reads=[pfx + "wsl", pfx + "xqfm"], writes=["psA", "psU"],
                   tag=f"{pfx}h.m{ic}")

            def vsc(e, inc, mk, ic=ic):
                e.tensor_mul(scpp[:, 0:1], scli[:, ic:ic + 1], repA[:, 1:2])
                mk(e.tensor_mul(scpp[:, 1:2], scli[:, NI + ic:NI + ic + 1], repA[:, 1:2]))
            c.step("ve", vsc, reads=[pfx + "scli", "repA"], writes=[pfx + "scpp"],
                   tag=f"{pfx}h.s{ic}")
            c.step("se", lambda e, inc, mk: mk(e.activation(
                out=gate[:, :], in_=psA[:, :], func=AF.Silu, scale=scpp[:, 0:1])),
                reads=["psA", pfx + "scpp"], writes=[pfx + "gate"], tag=f"{pfx}h.g{ic}")

            c.step("ve", lambda e, inc, mk, ic=ic: mk(e.scalar_tensor_tensor(
                hsb[:, :], psU[:, :], scpp[:, 1:2], gate[:, :], ALU.mult, ALU.mult)),
                reads=["psU", pfx + "gate", pfx + "scpp"], writes=[pfx + "hsb"],
                tag=f"{pfx}h.h{ic}")
            c.step("ve", lambda e, inc, mk, ic=ic: mk(e.reduce_max(
                red[:, 1:2], hsb[:, :], AX.X, apply_absolute_value=True)),
                reads=[pfx + "hsb"], writes=["redrow"], tag=f"{pfx}h.hm{ic}")
            c.step("ve", lambda e, inc, mk, ic=ic: mk(e.tensor_max(
                red[:, 0:1], red[:, 0:1], red[:, 1:2])),
                writes=["redrow"], tag=f"{pfx}h.hx{ic}")
            c.step("gp", lambda e, inc, mk, ic=ic: inc(e.dma_start(
                out=hbuf[ic * 128:(ic + 1) * 128, :], in_=hsb[:, :])),
                reads=[pfx + "hsb"], writes=[r_hbuf], weight=16, tag=f"{pfx}h.x{ic}")

    emit_allreduce_max(c, pfx + "_cch", 3)
    emit_quant_scale(c, 3, 3, 2)
    emit_replicate(c, 2, 2)
    emit_replicate(c, 3, 3)

    with scoped(c):
        hq = c.sb(pfx + "hq", [128, NI * 512], BF16)
        wdsl = c.sb(pfx + "wdsl", [128, D], BF16)
        gsb = c.sb(pfx + "gsb", [128, 512])
        ssl = c.sb(pfx + "ssl", [128, 512])
        hts = c.sb(pfx + "hts", [128, 512])
        pfull = c.sb(pfx + "pfull", [128, DC * 512])

        for tc in range(4):
            for ic in range(NI):
                c.step("sy", lambda e, inc, mk, ic=ic, tc=tc: inc(e.dma_start(
                    out=hts[:, 0:512],
                    in_=hbuf[ic * 128:(ic + 1) * 128, tc * 512:(tc + 1) * 512])),
                    reads=[r_hbuf], writes=[pfx + "hts"], weight=16, tag=f"{pfx}d.h{ic}")
                c.step("se", lambda e, inc, mk: mk(e.activation(
                    out=gsb[:, 0:512], in_=hts[:, 0:512], func=AF.Identity,
                    scale=repA[:, 2:3])),
                    reads=[pfx + "hts", "repA"], writes=[pfx + "gsb"], tag=f"{pfx}d.q1")

                c.step("ve", lambda e, inc, mk: mk(e.tensor_scalar(
                    gsb[:, 0:512], gsb[:, 0:512], MAGIC, MAGIC, ALU.add, ALU.subtract)),
                    writes=[pfx + "gsb"], tag=f"{pfx}d.q2a")
                c.step("ve", lambda e, inc, mk, ic=ic: mk(e.tensor_scalar(
                    hq[:, ic * 512:(ic + 1) * 512], gsb[:, 0:512],
                    -128.0, 127.0, ALU.max, ALU.min)),
                    reads=[pfx + "gsb"], writes=[pfx + "hq"], tag=f"{pfx}d.q2b")

            for ic in range(NI):
                c.step("sy", lambda e, inc, mk, ic=ic: inc(e.dma_start(
                    out=wdsl[:, :], in_=wdq_d[ic * 128:(ic + 1) * 128, :])),
                    reads=[r_wdq], writes=[pfx + "wdsl"], weight=16, tag=f"{pfx}d.w{ic}")

                def pdm(e, inc, mk, ic=ic):
                    for dch in range(DC):
                        dst = psA if dch < 4 else psU
                        off = (dch % 4) * 512
                        mk(e.matmul(dst[:, off:off + 512],
                                    wdsl[:, dch * 128:(dch + 1) * 128],
                                    hq[:, ic * 512:(ic + 1) * 512],
                                    start=(ic == 0), stop=(ic == NI - 1)))
                c.step("pe", pdm, reads=[pfx + "wdsl", pfx + "hq"],
                       writes=["psA", "psU"], tag=f"{pfx}d.m{ic}")

            for dch in range(DC):
                src = psA if dch < 4 else psU
                srcn = "psA" if dch < 4 else "psU"
                off = (dch % 4) * 512
                c.step("ve", lambda e, inc, mk, dch=dch: mk(e.tensor_mul(
                    scpp[:, 2:3], scli[:, 2 * NI + dch:2 * NI + dch + 1], repA[:, 3:4])),
                    reads=[pfx + "scli", "repA"], writes=[pfx + "scpp"], tag=f"{pfx}d.s")
                c.step("se", lambda e, inc, mk, src=src, off=off, srcn=srcn: mk(
                    e.activation(out=gsb[:, 0:512], in_=src[:, off:off + 512],
                                 func=AF.Silu, scale=scpp[:, 2:3])),
                    reads=[srcn, pfx + "scpp"], writes=[pfx + "gsb"], tag=f"{pfx}d.g")
                c.step("sy", lambda e, inc, mk, dch=dch, tc=tc: inc(e.dma_start(
                    out=ssl[:, 0:512],
                    in_=s_fm[dch * 128:(dch + 1) * 128, tc * 512:(tc + 1) * 512])),
                    reads=[s_fm_res], writes=[pfx + "ssl"], weight=16, tag=f"{pfx}d.sl")
                c.step("ve", lambda e, inc, mk, dch=dch: mk(e.tensor_mul(
                    pfull[:, dch * 512:(dch + 1) * 512], ssl[:, 0:512], gsb[:, 0:512])),
                    reads=[pfx + "ssl", pfx + "gsb"], writes=[pfx + "pfull"],
                    tag=f"{pfx}d.p")

            for dch in range(DC):
                def ptp(e, inc, mk, dch=dch):
                    for q in range(4):
                        mk(e.transpose(psA[:, q * 128:(q + 1) * 128],
                                       pfull[:, dch * 512 + q * 128:dch * 512 + (q + 1) * 128],
                                       c.cst["identf"][:, :]))
                c.step("pe", ptp, reads=[pfx + "pfull", "c_identf"], writes=["psA"],
                       tag=f"{pfx}d.t{dch}")

                def vcp(e, inc, mk, dch=dch, tc=tc):
                    for q in range(4):
                        tt = tc * 4 + q
                        ins = e.tensor_copy(pacc[:, tt * D + dch * 128:tt * D + (dch + 1) * 128],
                                            psA[:, q * 128:(q + 1) * 128])
                    mk(ins)
                c.step("ve", vcp, reads=["psA"], writes=[pfx + "pacc"], tag=f"{pfx}d.c")

    return xqfm, pacc


def emit_attn_l1(c, W, xnfm, s_fm, s_fm_res):
    nc = c.nc
    psA, psU, scal = c.t_psA, c.t_psU, c.t_scal
    idf, ones1 = c.cst["identf"], c.cst["ones128"]
    triT = c.cst["triT"]

    vtok = c.sb("a1_vtok", [128, NT * D], BF16)
    mrg = c.sb("a1_mrg", [128, DC * T], BF16)
    qh = c.sb("a1_qh", [128, T], BF16)
    kh = c.sb("a1_kh", [128, T], BF16)
    at = c.sb("a1_at", [128, T], BF16)
    sqq = c.sb("a1_sqq", [128, T], BF16)
    nsb = c.sb("a1_nsb", [64, T])
    rdr = c.sb("a1_rdr", [1, T])
    vh = c.sb("a1_vh", [128, 66], BF16)
    wbig = c.sb("a1_wbig", [128, DC * D], BF16)
    wqk = c.sb("a1_wqk", [128, DC * 256], BF16)
    vb_bc = c.sb("a1_vbbc", [128, D])
    brow = c.sb("a1_brow", [1, D])
    bqc = c.sb("a1_bqc", [128, 16])
    boc = c.sb("a1_boc", [128, 8])
    onescb = c.sb("a1_onescb", [128, 1], BF16)
    repB = c.sb("a1_repB", [128, 2])
    sfmt = c.sb("a1_sfmt", [128, T])

    c.step("ve", lambda e, inc, mk: mk(e.memset(onescb[:, :], 1.0)),
           writes=["a1_onescb"], tag="a1.ones")
    c.step("ve", lambda e, inc, mk: mk(e.memset(vh[:, 64:65], 1.0)),
           writes=["a1_vh"], tag="a1.vh1")

    def bias_cols(vec_ap, dst, dst_res, col0):
        c.step("sy", lambda e, inc, mk: inc(e.dma_start(
            out=brow[0:1, :], in_=vec_ap[None, :])),
            writes=["a1_brow"], weight=16, tag="a1.bl")

        def ptr(e, inc, mk):
            for ch in range(8):
                mk(e.transpose(psA[0:128, ch:ch + 1],
                               brow[0:1, ch * 128:(ch + 1) * 128], idf[0:1, 0:1]))
        c.step("pe", ptr, reads=["a1_brow", "c_identf"], writes=["psA"], tag="a1.bt")
        c.step("ve", lambda e, inc, mk: mk(e.tensor_copy(
            dst[:, col0:col0 + 8], psA[:, 0:8])),
            reads=["psA"], writes=[dst_res], tag="a1.bc")

    bias_cols(W["b_qkv"][0:1024], bqc, "a1_bqc", 0)
    bias_cols(W["b_qkv"][1024:2048], bqc, "a1_bqc", 8)
    bias_cols(W["b_out"], boc, "a1_boc", 0)

    c.step("sy", lambda e, inc, mk: inc(e.dma_start(
        out=brow[0:1, :], in_=W["b_qkv"][None, 2048:3072])),
        writes=["a1_brow"], weight=16, tag="a1.vbl")

    def pvb(e, inc, mk):
        for hf in range(2):
            mk(e.matmul(psA[:, hf * 512:(hf + 1) * 512], ones1[0:1, :],
                        brow[0:1, hf * 512:(hf + 1) * 512], start=True, stop=True))
    c.step("pe", pvb, reads=["a1_brow", "c_ones128"], writes=["psA"], tag="a1.vbm")
    c.step("ve", lambda e, inc, mk: mk(e.tensor_copy(vb_bc[:, :], psA[:, 0:D])),
           reads=["psA"], writes=["a1_vbbc"], tag="a1.vbc")

    def gwv(e, inc, mk):
        for ch in range(DC):
            inc(e.dma_start(out=wbig[:, ch * D:(ch + 1) * D],
                            in_=W["W_qkv"][ch * 128:(ch + 1) * 128, 2048:3072]))
    c.step("gp", gwv, writes=["a1_wbig"], weight=16 * DC, tag="a1.wv")

    for t in range(NT):
        def pv(e, inc, mk, t=t):
            for hf in range(2):
                for ch in range(DC):
                    mk(e.matmul(psA[:, hf * 512:(hf + 1) * 512],
                                xnfm[:, ch * T + t * 128:ch * T + (t + 1) * 128],
                                wbig[:, ch * D + hf * 512:ch * D + (hf + 1) * 512],
                                start=(ch == 0), stop=(ch == DC - 1)))
        c.step("pe", pv, reads=["a1_wbig", "l_xnfm"], writes=["psA"], tag=f"a1.v{t}")

        def vv(e, inc, mk, t=t):
            mk(e.tensor_tensor(vtok[:, t * D:(t + 1) * D], psA[:, 0:D],
                               vb_bc[:, :], ALU.add))
        c.step("ve", vv, reads=["psA", "a1_vbbc"], writes=["a1_vtok"], tag=f"a1.vc{t}")

    for p in range(DC):
        def gqk(e, inc, mk, p=p):
            inc(e.dma_start(
                out=wqk[:, 0:D].rearrange("q (ch j) -> q ch j", j=128),
                in_=bass.AP(tensor=W["W_qkv"].tensor, offset=p * 128,
                            ap=[[3072, 128], [128 * 3072, DC], [1, 128]])))
            inc(e.dma_start(
                out=wqk[:, D:2 * D].rearrange("q (ch j) -> q ch j", j=128),
                in_=bass.AP(tensor=W["W_qkv"].tensor, offset=1024 + p * 128,
                            ap=[[3072, 128], [128 * 3072, DC], [1, 128]])))
        c.step("gp", gqk, writes=["a1_wqk"], weight=32, tag=f"a1.qk{p}")

        for which, dst, dstn in (("q", qh, "a1_qh"), ("k", kh, "a1_kh")):
            woff = 0 if which == "q" else D

            def pp(e, inc, mk, woff=woff):
                for n4 in range(4):
                    for ch in range(DC):
                        mk(e.matmul(psA[:, n4 * 512:(n4 + 1) * 512],
                                    wqk[:, woff + ch * 128:woff + (ch + 1) * 128],
                                    xnfm[:, ch * T + n4 * 512:ch * T + (n4 + 1) * 512],
                                    start=(ch == 0), stop=(ch == DC - 1)))
            c.step("pe", pp, reads=["a1_wqk", "l_xnfm"], writes=["psA"],
                   tag=f"a1.{which}p{p}")
            if which == "q":
                c.step("ve", lambda e, inc, mk, p=p: mk(e.tensor_scalar(
                    qh[:, :], psA[:, :], bqc[:, p:p + 1], None, ALU.add)),
                    reads=["psA", "a1_bqc"], writes=["a1_qh"], tag=f"a1.qe{p}")
            else:
                c.step("ve", lambda e, inc, mk, p=p: mk(e.tensor_scalar(
                    kh[:, :], psA[:, :], bqc[:, 8 + p:9 + p], 0.125,
                    ALU.add, ALU.mult)),
                    reads=["psA", "a1_bqc"], writes=["a1_kh"], tag=f"a1.ke{p}")

        for which, src, srcn in (("q", qh, "a1_qh"), ("k", kh, "a1_kh")):
            c.step("se", lambda e, inc, mk, src=src, srcn=srcn: mk(e.activation(
                out=sqq[:, :], in_=src[:, :], func=AF.Square)),
                reads=[srcn], writes=["a1_sqq"], tag="a1.sq")

            coff = 0 if which == "q" else 2
            for hh in range(2):
                def pb(e, inc, mk, hh=hh):
                    for k4 in range(4):
                        mk(e.matmul(psA[0:1, k4 * 512:(k4 + 1) * 512],
                                    onescb[hh * 64:hh * 64 + 64, 0:1],
                                    sqq[hh * 64:hh * 64 + 64, k4 * 512:(k4 + 1) * 512],
                                    start=True, stop=True))
                c.step("pe", pb, reads=["a1_sqq", "a1_onescb"], writes=["psA"], tag="a1.bm")
                c.step("ve", lambda e, inc, mk, hh=hh, coff=coff: mk(e.reduce_max(
                    scal[0:1, 4 + coff + hh:5 + coff + hh], psA[0:1, 0:T], AX.X)),
                    reads=["psA"], writes=["scal"], tag="a1.br")

        c.step("se", lambda e, inc, mk: mk(e.activation(
            out=scal[0:1, 4:8], in_=scal[0:1, 4:8], func=AF.Sqrt)),
            reads=["scal"], writes=["scal"], tag="a1.bs")

        def vc(e, inc, mk):
            e.tensor_mul(scal[0:1, 8:9], scal[0:1, 4:5], scal[0:1, 6:7])
            mk(e.tensor_mul(scal[0:1, 9:10], scal[0:1, 5:6], scal[0:1, 7:8]))
        c.step("ve", vc, reads=["scal"], writes=["scal"], tag="a1.bc2")
        c.step("ve", lambda e, inc, mk: mk(e.tensor_scalar(
            scal[0:1, 8:10], scal[0:1, 8:10], -1.0, None, ALU.mult)),
            writes=["scal"], tag="a1.bc3")

        def prep(e, inc, mk):
            for hh in range(2):
                mk(e.matmul(psA[:, hh * 512:hh * 512 + 1], ones1[0:1, :],
                            scal[0:1, 8 + hh:9 + hh], start=True, stop=True))
        c.step("pe", prep, reads=["scal", "c_ones128"], writes=["psA"], tag="a1.brep")
        def brc(e, inc, mk):
            e.tensor_copy(repB[:, 0:1], psA[:, 0:1])
            mk(e.tensor_copy(repB[:, 1:2], psA[:, 512:513]))
        c.step("ve", brc, reads=["psA"], writes=["a1_repB"], tag="a1.brc")

        for hh in range(2):
            h = 2 * p + hh
            base = hh * 64
            for rt in range(NT):
                l0 = rt * 128

                def pst(e, inc, mk, rt=rt, l0=l0, base=base):
                    for k4 in range(4):
                        lo = max(512 * k4, l0)
                        n = 512 * k4 + 512 - lo
                        if n <= 0:
                            continue
                        mk(e.matmul(psA[:, lo:lo + n],
                                    kh[base:base + 64, rt * 128:(rt + 1) * 128],
                                    qh[base:base + 64, lo:lo + n],
                                    start=True, stop=True))
                c.step("pe", pst, reads=["a1_qh", "a1_kh"], writes=["psA"],
                       tag=f"a1.st{h}.{rt}")

                def sex(e, inc, mk, rt=rt, l0=l0, hh=hh):
                    for k4 in range(4):
                        lo = max(512 * k4, l0)
                        n = 512 * k4 + 512 - lo
                        if n <= 0:
                            continue
                        mk(e.activation(out=at[:, lo:lo + n], in_=psA[:, lo:lo + n],
                                        func=AF.Exp, bias=repB[:, hh:hh + 1]))
                c.step("se", sex, reads=["psA", "a1_repB"], writes=["a1_at"],
                       tag=f"a1.ex{h}.{rt}")
                c.step("ve", lambda e, inc, mk, l0=l0: mk(e.tensor_mul(
                    at[:, l0:l0 + 128], at[:, l0:l0 + 128], triT[:, :])),
                    reads=["c_triT"], writes=["a1_at"], tag=f"a1.msk{h}.{rt}")
                c.step("ve", lambda e, inc, mk, rt=rt, h=h: mk(e.tensor_copy(
                    vh[:, 0:64], vtok[:, rt * D + 64 * h:rt * D + 64 * h + 64])),
                    reads=["a1_vtok"], writes=["a1_vh"], tag=f"a1.vh{h}.{rt}")

                def pav(e, inc, mk, rt=rt, l0=l0):
                    for k4 in range(4):
                        lo = max(512 * k4, l0)
                        n = 512 * k4 + 512 - lo
                        if n <= 0:
                            continue
                        mk(e.matmul(psU[0:65, lo:lo + n], vh[:, 0:65],
                                    at[:, lo:lo + n],
                                    start=(rt == 0), stop=(rt == min(4 * k4 + 3, NT - 1))))
                c.step("pe", pav, reads=["a1_vh", "a1_at"], writes=["psU"],
                       tag=f"a1.av{h}.{rt}")

            c.step("ve", lambda e, inc, mk: mk(e.tensor_copy(nsb[:, :], psU[0:64, :])),
                   reads=["psU"], writes=["a1_nsb"], tag=f"a1.nm{h}")
            c.step("se", lambda e, inc, mk: mk(e.activation(
                out=rdr[0:1, :], in_=psU[64:65, :], func=AF.Ln)),
                reads=["psU"], writes=["a1_rdr"], tag=f"a1.rd{h}")
            c.step("se", lambda e, inc, mk: mk(e.activation(
                out=rdr[0:1, :], in_=rdr[0:1, :], func=AF.Exp, scale=-1.0)),
                writes=["a1_rdr"], tag=f"a1.rd2{h}")

            def pbc2(e, inc, mk):
                for k4 in range(4):
                    mk(e.matmul(psU[0:64, k4 * 512:(k4 + 1) * 512], ones1[0:1, 0:64],
                                rdr[0:1, k4 * 512:(k4 + 1) * 512], start=True, stop=True))
            c.step("pe", pbc2, reads=["a1_rdr", "c_ones128"], writes=["psU"],
                   tag=f"a1.rb{h}")
            c.step("ve", lambda e, inc, mk, p=p, base=base: mk(e.tensor_tensor(
                mrg[base:base + 64, p * T:(p + 1) * T], nsb[:, :], psU[0:64, :],
                ALU.mult)),
                reads=["a1_nsb", "psU"], writes=["a1_mrg"], tag=f"a1.mg{h}")

    def gwo(e, inc, mk):
        for ch in range(DC):
            inc(e.dma_start(out=wbig[:, ch * D:(ch + 1) * D],
                            in_=W["W_out"][ch * 128:(ch + 1) * 128, :]))
    c.step("gp", gwo, writes=["a1_wbig"], weight=16 * DC, tag="a1.wo")

    for ech in range(DC):
        def po(e, inc, mk, ech=ech):
            for n4 in range(4):
                for ch in range(DC):
                    mk(e.matmul(psA[:, n4 * 512:(n4 + 1) * 512],
                                wbig[:, ch * D + ech * 128:ch * D + (ech + 1) * 128],
                                mrg[:, ch * T + n4 * 512:ch * T + (n4 + 1) * 512],
                                start=(ch == 0), stop=(ch == DC - 1)))
        c.step("pe", po, reads=["a1_wbig", "a1_mrg"], writes=["psA"], tag=f"a1.o{ech}")
        c.step("ve", lambda e, inc, mk, ech=ech: mk(e.tensor_scalar(
            sfmt[:, :], psA[:, :], boc[:, ech:ech + 1], None, ALU.add)),
            reads=["psA", "a1_boc"], writes=["a1_sfmt"], tag=f"a1.oe{ech}")
        c.step("gp", lambda e, inc, mk, ech=ech: inc(e.dma_start(
            out=s_fm[ech * 128:(ech + 1) * 128, :], in_=sfmt[:, :])),
            reads=["a1_sfmt"], writes=[s_fm_res], weight=16, tag=f"a1.os{ech}")


def build_layer1(nc, dbg=False):
    c = Ctx(nc)
    x_in = nc.dram_tensor("x_in", [T, D], F32, kind="ExternalInput").ap()
    x_out = nc.dram_tensor("x_out", [T, D], F32, kind="ExternalOutput").ap()
    wnames = {
        "W_qkv": [D, 3072], "b_qkv": [3072], "W_out": [D, D], "b_out": [D],
        "clip_x": [1], "clip_down": [1],
        "w_gate": [2816, D], "w_up": [2816, D], "w_down": [D, 2816],
        "w_gate_t": [D, 2816], "w_up_t": [D, 2816], "w_down_t": [2816, D],
    }
    W = {k: nc.dram_tensor("w_" + k, shp, F32, kind="ExternalInput").ap()
         for k, shp in wnames.items()}
    s_fm = nc.dram_tensor("s_fm", [D, T], F32,
                          kind="ExternalOutput" if dbg else "Internal")
    declare_common(c)
    r_sfm = c.newres("r_sfm")
    r_xout = c.newres("r_xout")

    def _ldclips(e, inc, mk):
        inc(e.dma_start(out=c.t_tiny[0:1, 0:1], in_=W["clip_x"][None, :]))
        inc(e.dma_start(out=c.t_tiny[0:1, 1:2], in_=W["clip_down"][None, :]))
    c.step("sy", _ldclips, writes=["tiny"], weight=32, tag="clips")
    c.step("se", lambda e, inc, mk: mk(e.activation(
        out=c.t_tiny[0:1, 2:4], in_=c.t_tiny[0:1, 0:2], func=AF.Sigmoid)),
        reads=["tiny"], writes=["tiny"], tag="sig")

    with scoped(c):
        xnfm = c.sb("l_xnfm", [128, DC * T], BF16)
        emit_ln_pass(c, x_in, xnfm, "l_xnfm")
        emit_allreduce_max(c, "l1_ccx", 1)
        emit_quant_scale(c, 2, 1, 0)
        emit_replicate(c, 0, 0)
        emit_replicate(c, 1, 1)
        emit_attn_l1(c, W, xnfm, s_fm.ap(), r_sfm)

    if DBG_FFN:
        dbg = nc.dram_tensor("dbgs", [4, 128], F32, kind="ExternalOutput").ap()
        rdbg = c.newres("rdbg")
        def _dst(e, inc, mk):
            inc(e.dma_start(out=dbg[0:1, 0:16], in_=c.t_scal[0:1, :]))
            inc(e.dma_start(out=dbg[1:2, 0:6], in_=c.t_repA[0:1, 0:6]))
            inc(e.dma_start(out=dbg[2:3, :], in_=c.t_row1[0:1, :]))
            inc(e.dma_start(out=dbg[3:4, 0:2], in_=c.t_red[0:1, 0:2]))
        c.step("gp", _dst, reads=["scal", "repA", "row1", "redrow"], writes=[rdbg],
               weight=64, tag="dbg")
    xq_t, pacc_t = emit_swiglu_ffn(c, "f1", W, x_in, s_fm.ap(), r_sfm)
    emit_residual_combine(c, x_in, x_out, r_xout, pacc_t)
    c.step("gp", lambda e, inc, mk: None, reads=[r_xout], tag="final")
    return c


def kernel(**inputs):
    raise NotImplementedError
def emit_attn_l2(c, W, xnfm, s_fm, s_fm_res, stage=99):
    nc = c.nc
    psA, psU, scal = c.t_psA, c.t_psU, c.t_scal
    idf, ones1 = c.cst["identf"], c.cst["ones128"]
    EPS = 1e-6

    vtok = c.sb("a2_vtok", [128, NT * D], BF16)
    mrg = c.sb("a2_mrg", [128, DC * T], BF16)
    qk = c.sb("a2_qk", [128, T], BF16)
    sqq = c.sb("a2_sqq", [128, T], BF16)
    phK = c.sb("a2_phK", [128, NT * 256], BF16)
    phQT = c.sb("a2_phQT", [128, 2 * T], BF16)
    ph32 = c.sb("a2_ph32", [128, 256])
    q2t = c.sb("a2_q2t", [128, 2 * NT])
    mx2 = c.sb("a2_mx2", [128, 4])
    prjt = c.sb("a2_prjt", [128, 256], BF16)
    vh = c.sb("a2_vh", [128, 66], BF16)
    kvs = c.sb("a2_kvs", [128, 130], BF16)
    nsb = c.sb("a2_nsb", [64, T])
    rdr = c.sb("a2_rdr", [1, T])
    wbig = c.sb("a2_wbig", [128, DC * D], BF16)
    wqk2 = c.sb("a2_wqk2", [128, DC * 128], BF16)
    onescb = c.sb("a2_onescb", [128, 1], BF16)
    boc = c.sb("a2_boc", [128, 8])
    sfmt = c.sb("a2_sfmt", [128, T])

    c.step("ve", lambda e, inc, mk: mk(e.memset(onescb[:, :], 1.0)),
           writes=["a2_onescb"], tag="a2.on")
    c.step("ve", lambda e, inc, mk: mk(e.memset(vh[:, 64:65], 1.0)),
           writes=["a2_vh"], tag="a2.vh1")
    emit_bias_cols(c, W["bo"], D, boc, "a2_boc")

    def gwv(e, inc, mk):
        for ch in range(DC):
            inc(e.dma_start(out=wbig[:, ch * D:(ch + 1) * D],
                            in_=W["wv_t"][ch * 128:(ch + 1) * 128, :]))
    c.step("gp", gwv, writes=["a2_wbig"], weight=16 * DC, tag="a2.wv")
    for t in range(NT):
        def pv(e, inc, mk, t=t):
            for hf in range(2):
                for ch in range(DC):
                    mk(e.matmul(psA[:, hf * 512:(hf + 1) * 512],
                                xnfm[:, ch * T + t * 128:ch * T + (t + 1) * 128],
                                wbig[:, ch * D + hf * 512:ch * D + (hf + 1) * 512],
                                start=(ch == 0), stop=(ch == DC - 1)))
        c.step("pe", pv, reads=["a2_wbig", "l_xnfm"], writes=["psA"], tag=f"a2.v{t}")
        c.step("ve", lambda e, inc, mk, t=t: mk(e.tensor_copy(
            vtok[:, t * D:(t + 1) * D], psA[:, 0:D])),
            reads=["psA"], writes=["a2_vtok"], tag=f"a2.vc{t}")

    if stage == 0:
        for ech in range(DC):
            c.step("gp", lambda e, inc, mk, ech=ech: inc(e.dma_start(
                out=s_fm[ech * 128:(ech + 1) * 128, :], in_=sfmt[:, :])),
                reads=["a2_sfmt"], writes=[s_fm_res], weight=16, tag="a2.dummy")
        return
    for h in range(16 if stage >= 3 else 1):
        p, hh = divmod(h, 2)
        base = hh * 64
        def gqk(e, inc, mk, h=h):
            inc(e.dma_start(
                out=wqk2[:, :].rearrange("q (ch j) -> q ch j", j=128)[:, :, 0:64],
                in_=bass.AP(tensor=W["wq_t"].tensor, offset=64 * h,
                            ap=[[D, 128], [128 * D, DC], [1, 64]])))
            inc(e.dma_start(
                out=wqk2[:, :].rearrange("q (ch j) -> q ch j", j=128)[:, :, 64:128],
                in_=bass.AP(tensor=W["wk_t"].tensor, offset=64 * h,
                            ap=[[D, 128], [128 * D, DC], [1, 64]])))
            inc(e.dma_start(out=prjt[0:64, :], in_=W["proj_t"][h, :, :]))
            inc(e.dma_start(out=prjt[64:128, :], in_=W["proj_t"][h, :, :]))
        c.step("gp", gqk, writes=["a2_wqk2", "a2_prjt"], weight=64, tag=f"a2.qk{h}")

        def pqk(e, inc, mk):
            for half, off in ((0, 0), (1, 64)):
                for n4 in range(4):
                    for ch in range(DC):
                        mk(e.matmul(psA[half * 64:half * 64 + 64,
                                        n4 * 512:(n4 + 1) * 512],
                                    wqk2[:, ch * 128 + off:ch * 128 + off + 64],
                                    xnfm[:, ch * T + n4 * 512:ch * T + (n4 + 1) * 512],
                                    start=(ch == 0), stop=(ch == DC - 1)))
        c.step("pe", pqk, reads=["a2_wqk2", "l_xnfm"], writes=["psA"], tag=f"a2.p{h}")
        c.step("ve", lambda e, inc, mk: mk(e.tensor_scalar(
            qk[:, :], psA[:, :], 0.125, None, ALU.mult)),
            reads=["psA"], writes=["a2_qk"], tag=f"a2.qe{h}")
        c.step("se", lambda e, inc, mk: mk(e.activation(
            out=sqq[:, :], in_=qk[:, :], func=AF.Square)),
            reads=["a2_qk"], writes=["a2_sqq"], tag=f"a2.sq{h}")

        if stage == 11:
            continue
        def psq(e, inc, mk):
            for t in range(NT):
                mk(e.matmul(psU[:, t:t + 1], sqq[0:64, t * 128:(t + 1) * 128],
                            onescb[0:64, 0:1], start=True, stop=True))
            for t in range(NT):
                mk(e.matmul(psU[:, NT + t:NT + t + 1],
                            sqq[64:128, t * 128:(t + 1) * 128],
                            onescb[64:128, 0:1], start=True, stop=True))
        c.step("pe", psq, reads=["a2_sqq", "a2_onescb"], writes=["psU"], tag=f"a2.s2{h}")
        c.step("ve", lambda e, inc, mk: mk(e.tensor_copy(
            q2t[:, :], psU[:, 0:2 * NT])),
            reads=["psU"], writes=["a2_q2t"], tag=f"a2.s2c{h}")

        if stage == 12:
            continue
        for t in range(NT):
            def pxw(e, inc, mk, t=t):
                mk(e.matmul(psA[:, 0:256], qk[0:64, t * 128:(t + 1) * 128],
                            prjt[0:64, :], start=True, stop=True))
                mk(e.matmul(psA[:, 512:768], qk[64:128, t * 128:(t + 1) * 128],
                            prjt[64:128, :], start=True, stop=True))
            c.step("pe", pxw, reads=["a2_qk", "a2_prjt"], writes=["psA"],
                   tag=f"a2.xw{h}.{t}")

            if stage == 130:
                continue
            def vmx(e, inc, mk, t=t):
                e.reduce_max(mx2[:, 0:1], psA[:, 0:256], AX.X)
                mk(e.reduce_max(mx2[:, 1:2], psA[:, 512:768], AX.X))
            c.step("ve", vmx, reads=["psA"], writes=["a2_mx2"], tag=f"a2.mx{h}.{t}")

            if stage == 131:
                continue
            def vbias(e, inc, mk, t=t):
                e.scalar_tensor_tensor(mx2[:, 2:3], q2t[:, t:t + 1], -0.5,
                                       mx2[:, 0:1], ALU.mult, ALU.subtract)
                mk(e.scalar_tensor_tensor(mx2[:, 3:4], q2t[:, NT + t:NT + t + 1],
                                          -0.5, mx2[:, 1:2], ALU.mult, ALU.subtract))
            c.step("ve", vbias, reads=["a2_q2t"], writes=["a2_mx2"], tag=f"a2.bi{h}.{t}")

            if stage == 13:
                continue
            c.step("se", lambda e, inc, mk, t=t: mk(e.activation(
                out=ph32[:, 0:256], in_=psA[:, 0:256], func=AF.Exp, bias=mx2[:, 2:3])),
                reads=["psA", "a2_mx2"], writes=["a2_ph32"], tag=f"a2.eq{h}.{t}")
            c.step("ve", lambda e, inc, mk: mk(e.tensor_scalar(
                ph32[:, 0:256], ph32[:, 0:256], 1e-6, 1.0 / 16.0, ALU.add, ALU.mult)),
                writes=["a2_ph32"], tag=f"a2.pq{h}.{t}")

            def ptq(e, inc, mk, t=t):
                for mc in range(2):
                    mk(e.transpose(psU[:, 512 + mc * 128:512 + (mc + 1) * 128],
                                   ph32[:, mc * 128:(mc + 1) * 128], idf[:, :]))
            c.step("pe", ptq, reads=["a2_ph32", "c_identf"], writes=["psU"],
                   tag=f"a2.tq{h}.{t}")

            def vtq(e, inc, mk, t=t):
                for mc in range(2):
                    ins = e.tensor_copy(
                        phQT[:, mc * T + t * 128:mc * T + (t + 1) * 128],
                        psU[:, 512 + mc * 128:512 + (mc + 1) * 128])
                mk(ins)
            c.step("ve", vtq, reads=["psU"], writes=["a2_phQT"], tag=f"a2.cq{h}.{t}")

            c.step("se", lambda e, inc, mk, t=t: mk(e.activation(
                out=ph32[:, 0:256], in_=psA[:, 512:768], func=AF.Exp, bias=mx2[:, 3:4])),
                reads=["psA", "a2_mx2"], writes=["a2_ph32"], tag=f"a2.ek{h}.{t}")
            c.step("ve", lambda e, inc, mk, t=t: mk(e.tensor_scalar(
                phK[:, t * 256:(t + 1) * 256], ph32[:, 0:256], 1e-6, 1.0 / 16.0,
                ALU.add, ALU.mult)),
                reads=["a2_ph32"], writes=["a2_phK"], tag=f"a2.pk{h}.{t}")

        if stage == 1:
            continue
        for mc in range(2):
            for t in range(NT):
                c.step("ve", lambda e, inc, mk, t=t, h=h: mk(e.tensor_copy(
                    vh[:, 0:64], vtok[:, t * D + 64 * h:t * D + 64 * h + 64])),
                    reads=["a2_vtok"], writes=["a2_vh"], tag=f"a2.kv.v{t}")
                c.step("pe", lambda e, inc, mk, t=t, mc=mc: mk(e.matmul(
                    psU[:, mc * 65:(mc + 1) * 65],
                    phK[:, t * 256 + mc * 128:t * 256 + (mc + 1) * 128],
                    vh[:, 0:65], start=(t == 0), stop=(t == NT - 1))),
                    reads=["a2_phK", "a2_vh"], writes=["psU"], tag=f"a2.kv{mc}.{t}")
        c.step("ve", lambda e, inc, mk: mk(e.tensor_copy(
            kvs[:, :], psU[:, 0:130])),
            reads=["psU"], writes=["a2_kvs"], tag=f"a2.kvc{h}")

        def pnum2(e, inc, mk):
            for mc in range(2):
                for k4 in range(4):
                    mk(e.matmul(psU[0:65, k4 * 512:(k4 + 1) * 512],
                                kvs[:, mc * 65:(mc + 1) * 65],
                                phQT[:, mc * T + k4 * 512:mc * T + (k4 + 1) * 512],
                                start=(mc == 0), stop=(mc == 1)))
        c.step("pe", pnum2, reads=["a2_phQT", "a2_kvs"], writes=["psU"],
               tag=f"a2.nm{h}")
        c.step("ve", lambda e, inc, mk: mk(e.tensor_copy(nsb[:, :], psU[0:64, :])),
               reads=["psU"], writes=["a2_nsb"], tag=f"a2.nc{h}")
        c.step("ve", lambda e, inc, mk: mk(e.tensor_scalar(
            rdr[0:1, :], psU[64:65, :], 1e-6, None, ALU.add)),
            reads=["psU"], writes=["a2_rdr"], tag=f"a2.rd{h}")
        c.step("se", lambda e, inc, mk: mk(e.activation(
            out=rdr[0:1, :], in_=rdr[0:1, :], func=AF.Ln)),
            writes=["a2_rdr"], tag=f"a2.rr{h}")
        c.step("se", lambda e, inc, mk: mk(e.activation(
            out=rdr[0:1, :], in_=rdr[0:1, :], func=AF.Exp, scale=-1.0)),
            writes=["a2_rdr"], tag=f"a2.rr2{h}")

        def pbc2(e, inc, mk):
            for k4 in range(4):
                mk(e.matmul(psU[0:64, k4 * 512:(k4 + 1) * 512], ones1[0:1, 0:64],
                            rdr[0:1, k4 * 512:(k4 + 1) * 512], start=True, stop=True))
        c.step("pe", pbc2, reads=["a2_rdr", "c_ones128"], writes=["psU"],
               tag=f"a2.rb{h}")
        c.step("ve", lambda e, inc, mk, p=p, base=base: mk(e.tensor_tensor(
            mrg[base:base + 64, p * T:(p + 1) * T], nsb[:, :], psU[0:64, :],
            ALU.mult)),
            reads=["a2_nsb", "psU"], writes=["a2_mrg"], tag=f"a2.mg{h}")

    if stage < 3:
        for ech in range(DC):
            c.step("gp", lambda e, inc, mk, ech=ech: inc(e.dma_start(
                out=s_fm[ech * 128:(ech + 1) * 128, :], in_=sfmt[:, :])),
                reads=["a2_sfmt"], writes=[s_fm_res], weight=16, tag="a2.dummy")
        return
    def gwo(e, inc, mk):
        for ch in range(DC):
            inc(e.dma_start(out=wbig[:, ch * D:(ch + 1) * D],
                            in_=W["wo_t"][ch * 128:(ch + 1) * 128, :]))
    c.step("gp", gwo, writes=["a2_wbig"], weight=16 * DC, tag="a2.wo")
    for ech in range(DC):
        def po(e, inc, mk, ech=ech):
            for n4 in range(4):
                for ch in range(DC):
                    mk(e.matmul(psA[:, n4 * 512:(n4 + 1) * 512],
                                wbig[:, ch * D + ech * 128:ch * D + (ech + 1) * 128],
                                mrg[:, ch * T + n4 * 512:ch * T + (n4 + 1) * 512],
                                start=(ch == 0), stop=(ch == DC - 1)))
        c.step("pe", po, reads=["a2_wbig", "a2_mrg"], writes=["psA"], tag=f"a2.o{ech}")
        c.step("ve", lambda e, inc, mk, ech=ech: mk(e.tensor_scalar(
            sfmt[:, :], psA[:, :], boc[:, ech:ech + 1], None, ALU.add)),
            reads=["psA", "a2_boc"], writes=["a2_sfmt"], tag=f"a2.oe{ech}")
        c.step("gp", lambda e, inc, mk, ech=ech: inc(e.dma_start(
            out=s_fm[ech * 128:(ech + 1) * 128, :], in_=sfmt[:, :])),
            reads=["a2_sfmt"], writes=[s_fm_res], weight=16, tag=f"a2.os{ech}")
def emit_gelu_ffn(c, pfx, W, x_in, s_fm, s_fm_res):
    nc = c.nc
    H = 4096
    NH = H // 128
    psA, psU = c.t_psA, c.t_psU
    scal, repA, red = c.t_scal, c.t_repA, c.t_red
    MINR = 1e-6

    hbuf = nc.dram_tensor(pfx + "_hbuf", [H, T], F32, kind="Internal")
    w1q_d = nc.dram_tensor(pfx + "_w1q", [D, H], BF16, kind="Internal")
    w2q_d = nc.dram_tensor(pfx + "_w2q", [H, D], BF16, kind="Internal")
    r_hbuf = c.newres(pfx + "hbuf")
    r_w1q = c.newres(pfx + "w1q")
    r_w2q = c.newres(pfx + "w2q")

    xqfm = c.sb(pfx + "xqfm", [128, DC * T], BF16)
    pacc = c.sb(pfx + "pacc", [128, NT * D])
    scpp = c.sb(pfx + "scpp", [128, 4])
    b1c = c.sb(pfx + "b1c", [128, NH])
    b2c = c.sb(pfx + "b2c", [128, DC])
    b1s = c.sb(pfx + "b1s", [128, NH])
    b2s = c.sb(pfx + "b2s", [128, DC])

    emit_scale_floor(c, 1, 0, MINR)
    emit_replicate(c, 0, 0)
    emit_quant_fm(c, x_in, xqfm, pfx + "xqfm", 0)
    emit_bias_cols(c, W["b1"], H, b1c, pfx + "b1c")
    emit_bias_cols(c, W["b2"], D, b2c, pfx + "b2c")

    with scoped(c):
        wnat = c.sb(pfx + "wnat", [128, H])
        wq16 = c.sb(pfx + "wq16", [128, H], BF16)

        for wkey, nch, width, gcol, icol in (("w1_t", DC, H, 10, 11),
                                             ("w2_t", NH, D, 12, 13)):
            c.step("ve", lambda e, inc, mk: mk(e.memset(red[:, 0:1], 0.0)),
                   writes=["redrow"], tag=f"{pfx}wm.z")
            for icx in range(nch):
                c.step("sy", lambda e, inc, mk, wkey=wkey, icx=icx, width=width: inc(
                    e.dma_start(out=wnat[:, 0:width],
                                in_=W[wkey][icx * 128:(icx + 1) * 128, :])),
                    writes=[pfx + "wnat"], weight=16, tag=f"{pfx}wm.l")
                c.step("ve", lambda e, inc, mk, width=width: mk(e.reduce_max(
                    red[:, 1:2], wnat[:, 0:width], AX.X, apply_absolute_value=True)),
                    reads=[pfx + "wnat"], writes=["redrow"], tag=f"{pfx}wm.r")
                c.step("ve", lambda e, inc, mk: mk(e.tensor_max(
                    red[:, 0:1], red[:, 0:1], red[:, 1:2])),
                    writes=["redrow"], tag=f"{pfx}wm.m")
            emit_local_absmax(c, pfx + "wm" + wkey, gcol)
            emit_scale_floor(c, gcol, icol, MINR)

        emit_replicate(c, 11, 6)
        emit_replicate(c, 13, 7)
        for wkey, dst_d, dst_r, nch, width, rcol in (
                ("w1_t", w1q_d, r_w1q, DC, H, 6), ("w2_t", w2q_d, r_w2q, NH, D, 7)):
            for rc in range(nch):
                c.step("sy", lambda e, inc, mk, wkey=wkey, rc=rc, width=width: inc(
                    e.dma_start(out=wnat[:, 0:width],
                                in_=W[wkey][rc * 128:(rc + 1) * 128, :])),
                    writes=[pfx + "wnat"], weight=16, tag=f"{pfx}qw.l")
                c.step("se", lambda e, inc, mk, width=width, rcol=rcol: mk(
                    e.activation(out=wnat[:, 0:width], in_=wnat[:, 0:width],
                                 func=AF.Identity, scale=repA[:, rcol:rcol + 1])),
                    reads=["repA"], writes=[pfx + "wnat"], tag=f"{pfx}qw.s")
                c.step("ve", lambda e, inc, mk, width=width: mk(e.tensor_scalar(
                    wnat[:, 0:width], wnat[:, 0:width], MAGIC, MAGIC,
                    ALU.add, ALU.subtract)),
                    writes=[pfx + "wnat"], tag=f"{pfx}qw.r")
                c.step("ve", lambda e, inc, mk, width=width: mk(e.tensor_scalar(
                    wq16[:, 0:width], wnat[:, 0:width], -128.0, 127.0,
                    ALU.max, ALU.min)),
                    reads=[pfx + "wnat"], writes=[pfx + "wq16"], tag=f"{pfx}qw.c")
                c.step("gp", lambda e, inc, mk, rc=rc, dst_d=dst_d, width=width: inc(
                    e.dma_start(out=dst_d[rc * 128:(rc + 1) * 128, :],
                                in_=wq16[:, 0:width])),
                    reads=[pfx + "wq16"], writes=[dst_r], weight=16,
                    tag=f"{pfx}qw.st")

    c.step("ve", lambda e, inc, mk: mk(e.tensor_mul(
        scal[0:1, 14:15], scal[0:1, 10:11], scal[0:1, 1:2])),
        reads=["scal"], writes=["scal"], tag=f"{pfx}s1")
    emit_replicate(c, 14, 4)
    c.step("ve", lambda e, inc, mk: mk(e.tensor_scalar(
        b1s[:, :], b1c[:, :], repA[:, 4:5], None, ALU.mult)),
        reads=[pfx + "b1c", "repA"], writes=[pfx + "b1s"], tag=f"{pfx}b1s")

    with scoped(c):
        wsl = c.sb(pfx + "wsl", [128, D], BF16)
        hsb = c.sb(pfx + "hsb", [128, T])

        c.step("ve", lambda e, inc, mk: mk(e.memset(red[:, 0:1], 0.0)),
               writes=["redrow"], tag=f"{pfx}h.z")
        for hc in range(NH):
            c.step("sy", lambda e, inc, mk, hc=hc: inc(e.dma_start(
                out=wsl[:, :].rearrange("p (ch j) -> p ch j", j=128),
                in_=bass.AP(tensor=w1q_d, offset=hc * 128,
                            ap=[[H, 128], [128 * H, DC], [1, 128]]))),
                reads=[r_w1q], writes=[pfx + "wsl"], weight=16, tag=f"{pfx}h.w{hc}")

            def ph(e, inc, mk, hc=hc):
                for n4 in range(4):
                    for ch in range(DC):
                        mk(e.matmul(psA[:, n4 * 512:(n4 + 1) * 512],
                                    wsl[:, ch * 128:(ch + 1) * 128],
                                    xqfm[:, ch * T + n4 * 512:ch * T + (n4 + 1) * 512],
                                    start=(ch == 0), stop=(ch == DC - 1)))
            c.step("pe", ph, reads=[pfx + "wsl", pfx + "xqfm"], writes=["psA"],
                   tag=f"{pfx}h.m{hc}")
            c.step("se", lambda e, inc, mk, hc=hc: mk(e.activation(
                out=hsb[:, :], in_=psA[:, :], func=AF.Gelu,
                scale=repA[:, 4:5], bias=b1s[:, hc:hc + 1])),
                reads=["psA", "repA", pfx + "b1s"], writes=[pfx + "hsb"],
                tag=f"{pfx}h.g{hc}")
            c.step("ve", lambda e, inc, mk: mk(e.reduce_max(
                red[:, 1:2], hsb[:, :], AX.X, apply_absolute_value=True)),
                reads=[pfx + "hsb"], writes=["redrow"], tag=f"{pfx}h.r{hc}")
            c.step("ve", lambda e, inc, mk: mk(e.tensor_max(
                red[:, 0:1], red[:, 0:1], red[:, 1:2])),
                writes=["redrow"], tag=f"{pfx}h.x{hc}")
            c.step("gp", lambda e, inc, mk, hc=hc: inc(e.dma_start(
                out=hbuf[hc * 128:(hc + 1) * 128, :], in_=hsb[:, :])),
                reads=[pfx + "hsb"], writes=[r_hbuf], weight=16, tag=f"{pfx}h.s{hc}")

    emit_allreduce_max(c, pfx + "_cch", 3)
    emit_scale_floor(c, 3, 2, MINR)
    emit_replicate(c, 2, 2)
    c.step("ve", lambda e, inc, mk: mk(e.tensor_mul(
        scal[0:1, 15:16], scal[0:1, 12:13], scal[0:1, 3:4])),
        reads=["scal"], writes=["scal"], tag=f"{pfx}s2")
    emit_replicate(c, 15, 5)
    c.step("ve", lambda e, inc, mk: mk(e.tensor_scalar(
        b2s[:, :], b2c[:, :], repA[:, 5:6], None, ALU.mult)),
        reads=[pfx + "b2c", "repA"], writes=[pfx + "b2s"], tag=f"{pfx}b2s")

    with scoped(c):
        hq = c.sb(pfx + "hq", [128, NH * 512], BF16)
        wdsl = c.sb(pfx + "wdsl", [128, D], BF16)
        gsb = c.sb(pfx + "gsb", [128, 512])
        ssl = c.sb(pfx + "ssl", [128, 512])
        hts = c.sb(pfx + "hts", [128, 512])
        pfull = c.sb(pfx + "pfull", [128, DC * 512])

        for tc in range(4):
            for hc in range(NH):
                c.step("sy", lambda e, inc, mk, hc=hc, tc=tc: inc(e.dma_start(
                    out=hts[:, 0:512],
                    in_=hbuf[hc * 128:(hc + 1) * 128, tc * 512:(tc + 1) * 512])),
                    reads=[r_hbuf], writes=[pfx + "hts"], weight=16, tag=f"{pfx}d.h")
                c.step("se", lambda e, inc, mk: mk(e.activation(
                    out=gsb[:, 0:512], in_=hts[:, 0:512], func=AF.Identity,
                    scale=repA[:, 2:3])),
                    reads=[pfx + "hts", "repA"], writes=[pfx + "gsb"], tag=f"{pfx}d.q1")
                c.step("ve", lambda e, inc, mk: mk(e.tensor_scalar(
                    gsb[:, 0:512], gsb[:, 0:512], MAGIC, MAGIC, ALU.add, ALU.subtract)),
                    writes=[pfx + "gsb"], tag=f"{pfx}d.q2")
                c.step("ve", lambda e, inc, mk, hc=hc: mk(e.tensor_scalar(
                    hq[:, hc * 512:(hc + 1) * 512], gsb[:, 0:512],
                    -128.0, 127.0, ALU.max, ALU.min)),
                    reads=[pfx + "gsb"], writes=[pfx + "hq"], tag=f"{pfx}d.q3")

            for hc in range(NH):
                c.step("sy", lambda e, inc, mk, hc=hc: inc(e.dma_start(
                    out=wdsl[:, :], in_=w2q_d[hc * 128:(hc + 1) * 128, :])),
                    reads=[r_w2q], writes=[pfx + "wdsl"], weight=16, tag=f"{pfx}d.w")

                def pdm(e, inc, mk, hc=hc):
                    for dch in range(DC):
                        dst = psA if dch < 4 else psU
                        off = (dch % 4) * 512
                        mk(e.matmul(dst[:, off:off + 512],
                                    wdsl[:, dch * 128:(dch + 1) * 128],
                                    hq[:, hc * 512:(hc + 1) * 512],
                                    start=(hc == 0), stop=(hc == NH - 1)))
                c.step("pe", pdm, reads=[pfx + "wdsl", pfx + "hq"],
                       writes=["psA", "psU"], tag=f"{pfx}d.m")

            for dch in range(DC):
                src = psA if dch < 4 else psU
                srcn = "psA" if dch < 4 else "psU"
                off = (dch % 4) * 512
                c.step("se", lambda e, inc, mk, src=src, off=off, srcn=srcn, dch=dch:
                       mk(e.activation(out=gsb[:, 0:512], in_=src[:, off:off + 512],
                                       func=AF.Silu, scale=repA[:, 5:6],
                                       bias=b2s[:, dch:dch + 1])),
                       reads=[srcn, "repA", pfx + "b2s"], writes=[pfx + "gsb"],
                       tag=f"{pfx}d.g")
                c.step("sy", lambda e, inc, mk, dch=dch, tc=tc: inc(e.dma_start(
                    out=ssl[:, 0:512],
                    in_=s_fm[dch * 128:(dch + 1) * 128, tc * 512:(tc + 1) * 512])),
                    reads=[s_fm_res], writes=[pfx + "ssl"], weight=16, tag=f"{pfx}d.sl")
                c.step("ve", lambda e, inc, mk, dch=dch: mk(e.tensor_mul(
                    pfull[:, dch * 512:(dch + 1) * 512], ssl[:, 0:512], gsb[:, 0:512])),
                    reads=[pfx + "ssl", pfx + "gsb"], writes=[pfx + "pfull"],
                    tag=f"{pfx}d.p")

            for dch in range(DC):
                def ptp(e, inc, mk, dch=dch):
                    for q in range(4):
                        mk(e.transpose(psA[:, q * 128:(q + 1) * 128],
                                       pfull[:, dch * 512 + q * 128:dch * 512 + (q + 1) * 128],
                                       c.cst["identf"][:, :]))
                c.step("pe", ptp, reads=[pfx + "pfull", "c_identf"], writes=["psA"],
                       tag=f"{pfx}d.t")

                def vcp(e, inc, mk, dch=dch, tc=tc):
                    for q in range(4):
                        tt = tc * 4 + q
                        ins = e.tensor_copy(
                            pacc[:, tt * D + dch * 128:tt * D + (dch + 1) * 128],
                            psA[:, q * 128:(q + 1) * 128])
                    mk(ins)
                c.step("ve", vcp, reads=["psA"], writes=[pfx + "pacc"], tag=f"{pfx}d.c")

    return xqfm, pacc


def build_layer2(nc, dbg=False):
    c = Ctx(nc)
    x_in = nc.dram_tensor("x_in", [T, D], F32, kind="ExternalInput").ap()
    x_out = nc.dram_tensor("x_out", [T, D], F32, kind="ExternalOutput").ap()
    wnames = {
        "wq_t": [D, D], "wk_t": [D, D], "wv_t": [D, D], "wo_t": [D, D],
        "bo": [D], "proj_t": [16, 64, 256],
        "w1_t": [D, 4096], "b1": [4096], "w2_t": [4096, D], "b2": [D],
    }
    W = {k: nc.dram_tensor("w_" + k, shp, F32, kind="ExternalInput").ap()
         for k, shp in wnames.items()}
    s_fm = nc.dram_tensor("s_fm", [D, T], F32,
                          kind="ExternalOutput" if dbg else "Internal")
    declare_common(c)
    r_sfm = c.newres("r_sfm")
    r_xout = c.newres("r_xout")

    with scoped(c):
        xnfm = c.sb("l_xnfm", [128, DC * T], BF16)
        emit_ln_pass(c, x_in, xnfm, "l_xnfm")
        emit_allreduce_max(c, "l2_ccx", 1)
        emit_attn_l2(c, W, xnfm, s_fm.ap(), r_sfm, stage=L2_STAGE)

    if L2_ATTN_ONLY:
        c.step("gp", lambda e, inc, mk: inc(e.dma_start(
            out=x_out[0:128, :], in_=c.t_xt[:, :])),
            reads=["xt"], writes=[r_xout], weight=16, tag="dummyout")
        c.step("gp", lambda e, inc, mk: None, reads=[r_xout, r_sfm], tag="final")
        return c
    xq_t, pacc_t = emit_gelu_ffn(c, "f2", W, x_in, s_fm.ap(), r_sfm)
    emit_residual_combine(c, x_in, x_out, r_xout, pacc_t, "f2pacc")
    c.step("gp", lambda e, inc, mk: None, reads=[r_xout], tag="final")
    return c
def emit_attn_l3(c, W, xnfm, s_fm, s_fm_res, pfx="a3", relv=True, bscale=None):
    nc = c.nc
    psA, psU, scal = c.t_psA, c.t_psU, c.t_scal
    idf, ones1 = c.cst["identf"], c.cst["ones128"]
    P2 = 2304
    PA = 4096
    B0 = 2048

    qrbuf = nc.dram_tensor(pfx + "_qrbufd", [2, 128 * (P2 + 1) + 2304], BF16, kind="Internal")
    abuf = nc.dram_tensor(pfx + "_abufd", [2, B0 + 2048 * (PA + 1) + 4096], BF16,
                          kind="Internal")
    r_qr = [c.newres(pfx + "_qr0"), c.newres(pfx + "_qr1")]
    r_ab = [c.newres(pfx + "_ab0"), c.newres(pfx + "_ab1")]

    vtok = c.sb(pfx + "_vtok", [128, NT * D], BF16)
    mrg = c.sb(pfx + "_mrg", [128, DC * T], BF16)
    qh = c.sb(pfx + "_qh", [128, T], BF16)
    kh = c.sb(pfx + "_kh", [128, T], BF16)
    sqq = c.sb(pfx + "_sqq", [128, T], BF16)
    at = c.sb(pfx + "_at", [128, T], BF16)
    atf = c.sb(pfx + "_atf", [128, T])
    bsb = c.sb(pfx + "_bsb", [128, T], BF16)
    qrw = c.sb(pfx + "_qrw", [128, 2304], BF16)
    rkp = c.sb(pfx + "_rkp", [128, 4096], BF16)
    rvp = c.sb(pfx + "_rvp", [128, 32 * 64], BF16)
    wjt = c.sb(pfx + "_wjt", [128, T], BF16)
    att = c.sb(pfx + "_att", [128, T], BF16)
    vh = c.sb(pfx + "_vh", [128, 66], BF16)
    nsb = c.sb(pfx + "_nsb", [64, T])
    wbig = c.sb(pfx + "_wbig", [128, DC * D], BF16)
    wqk = c.sb(pfx + "_wqk", [128, DC * 256], BF16)
    vb_bc = c.sb(pfx + "_vbbc", [128, D])
    bqc = c.sb(pfx + "_bqc", [128, 16])
    boc = c.sb(pfx + "_boc", [128, 8])
    onescb = c.sb(pfx + "_onescb", [128, 1], BF16)
    one65 = c.sb(pfx + "_one65", [1, 65], BF16)
    zc512 = c.sb(pfx + "_zc512", [1, 512], BF16)
    bndc = c.sb(pfx + "_bndc", [128, 4])
    rdr = atf[0:1, :]
    brow = c.t_xt[0:1, :]
    zrow = qrw
    sfmt = atf
    c.res[pfx + "_rdr"] = c.res[pfx + "_atf"]
    c.res[pfx + "_brow"] = c.res["xt"]
    c.res[pfx + "_zrow"] = c.res[pfx + "_qrw"]
    c.res[pfx + "_sfmt"] = c.res[pfx + "_atf"]

    if pfx == "a3":
        c.res_dump_at = at
        c.res_dump_bndc = bndc
        c.res_dump_nsb = nsb
        c.res_dump_rdr = atf
        c.res_dump_qrw = qrw
    c.step("ve", lambda e, inc, mk: mk(e.memset(onescb[:, :], 1.0)),
           writes=[pfx + "_onescb"], tag="a3.on")
    c.step("ve", lambda e, inc, mk: mk(e.memset(one65[:, :], 1.0)),
           writes=[pfx + "_one65"], tag="a3.o65")
    c.step("ve", lambda e, inc, mk: mk(e.memset(zc512[:, :], 0.0)),
           writes=[pfx + "_zc512"], tag="a3.zc")
    c.step("ve", lambda e, inc, mk: mk(e.memset(vh[:, 64:65], 1.0)),
           writes=[pfx + "_vh"], tag="a3.v1")
    c.step("ve", lambda e, inc, mk: mk(e.memset(zrow[:, 0:2304], 0.0)),
           writes=[pfx + "_zrow"], tag="a3.z")

    for buf in range(2):
        def zg(e, inc, mk, buf=buf):
            inc(e.dma_start(out=bass.AP(tensor=abuf, offset=buf * abuf.shape[1],
                                        ap=[[2048, 1], [1, 2048]]),
                            in_=zrow[0:1, 0:2048]))
            for lc in range(16):
                inc(e.dma_start(
                    out=bass.AP(tensor=abuf,
                                offset=buf * abuf.shape[1] + B0 + lc * 128 * (PA + 1) + 2048,
                                ap=[[PA + 1, 128], [1, 2049]]),
                    in_=zrow[:, 0:2049]))
            inc(e.dma_start(
                out=bass.AP(tensor=abuf,
                            offset=buf * abuf.shape[1] + B0 + 2048 * (PA + 1),
                            ap=[[2048, 2], [1, 2048]]),
                in_=zrow[0:2, 0:2048]))
        c.step("gp", zg, reads=[pfx + "_zrow"], writes=[r_ab[buf]], weight=16 * 18,
               tag=f"a3.zg{buf}")

    nrv = 32 if relv else 0
    def grk(e, inc, mk):
        inc(e.dma_start(out=rkp[0:64, :], in_=W["relk_pad"][:, :]))
        inc(e.dma_start(out=rkp[64:128, :], in_=W["relk_pad"][:, :]))
        for jc in range(nrv):
            inc(e.dma_start(out=rvp[:, jc * 64:(jc + 1) * 64],
                            in_=W["relv_pad"][jc * 128:(jc + 1) * 128, :]))
    c.step("gp", grk, writes=[pfx + "_rkp", pfx + "_rvp"], weight=16 * (2 + nrv),
           tag="a3.rk")

    c.step("se", lambda e, inc, mk: mk(e.activation(
        out=qrw[0:64, 0:2048], in_=rkp[0:64, 0:2048], func=AF.Square)),
        reads=[pfx + "_rkp"], writes=[pfx + "_qrw"], tag="a3.rn1")
    def prn(e, inc, mk):
        for k4 in range(4):
            mk(e.matmul(psA[0:1, k4 * 512:(k4 + 1) * 512], onescb[0:64, 0:1],
                        qrw[0:64, k4 * 512:(k4 + 1) * 512], start=True, stop=True))
    c.step("pe", prn, reads=[pfx + "_qrw", pfx + "_onescb"], writes=["psA"], tag="a3.rn2")
    c.step("se", lambda e, inc, mk: mk(e.activation(
        out=qrw[0:64, 0:2048], in_=rkp[0:64, 2048:4096], func=AF.Square)),
        reads=[pfx + "_rkp"], writes=[pfx + "_qrw"], tag="a3.rn3")
    def prn2(e, inc, mk):
        for k4 in range(4):
            mk(e.matmul(psU[0:1, k4 * 512:(k4 + 1) * 512], onescb[0:64, 0:1],
                        qrw[0:64, k4 * 512:(k4 + 1) * 512], start=True, stop=True))
    c.step("pe", prn2, reads=[pfx + "_qrw", pfx + "_onescb"], writes=["psU"], tag="a3.rn4")
    c.step("ve", lambda e, inc, mk: mk(e.reduce_max(
        scal[0:1, 4:5], psA[0:1, 0:2048], AX.X)),
        reads=["psA"], writes=["scal"], tag="a3.rn5")
    c.step("ve", lambda e, inc, mk: mk(e.reduce_max(
        scal[0:1, 5:6], psU[0:1, 0:2048], AX.X)),
        reads=["psU"], writes=["scal"], tag="a3.rn6")
    c.step("ve", lambda e, inc, mk: mk(e.tensor_max(
        scal[0:1, 4:5], scal[0:1, 4:5], scal[0:1, 5:6])),
        writes=["scal"], tag="a3.rn7")
    c.step("se", lambda e, inc, mk: mk(e.activation(
        out=scal[0:1, 4:5], in_=scal[0:1, 4:5], func=AF.Sqrt)),
        writes=["scal"], tag="a3.rn8")

    def bias_cols3(vec_ap, dst, dst_res, col0):
        c.step("sy", lambda e, inc, mk: inc(e.dma_start(
            out=brow[0:1, :], in_=vec_ap[None, :])),
            writes=[pfx + "_brow"], weight=16, tag="a3.bl")

        def ptr(e, inc, mk):
            for ch in range(8):
                mk(e.transpose(psA[0:128, ch:ch + 1],
                               brow[0:1, ch * 128:(ch + 1) * 128], idf[0:1, 0:1]))
        c.step("pe", ptr, reads=[pfx + "_brow", "c_identf"], writes=["psA"], tag="a3.bt")
        c.step("ve", lambda e, inc, mk: mk(e.tensor_copy(
            dst[:, col0:col0 + 8], psA[:, 0:8])),
            reads=["psA"], writes=[dst_res], tag="a3.bc")
    bias_cols3(W["bq"], bqc, pfx + "_bqc", 0)
    bias_cols3(W["bk"], bqc, pfx + "_bqc", 8)
    bias_cols3(W["bo"], boc, pfx + "_boc", 0)

    c.step("sy", lambda e, inc, mk: inc(e.dma_start(
        out=brow[0:1, :], in_=W["bv"][None, :])),
        writes=[pfx + "_brow"], weight=16, tag="a3.vbl")

    def pvb(e, inc, mk):
        for hf in range(2):
            mk(e.matmul(psA[:, hf * 512:(hf + 1) * 512], ones1[0:1, :],
                        brow[0:1, hf * 512:(hf + 1) * 512], start=True, stop=True))
    c.step("pe", pvb, reads=[pfx + "_brow", "c_ones128"], writes=["psA"], tag="a3.vbm")
    c.step("ve", lambda e, inc, mk: mk(e.tensor_copy(vb_bc[:, :], psA[:, 0:D])),
           reads=["psA"], writes=[pfx + "_vbbc"], tag="a3.vbc")

    def gwv(e, inc, mk):
        for ch in range(DC):
            inc(e.dma_start(out=wbig[:, ch * D:(ch + 1) * D],
                            in_=W["wv_t"][ch * 128:(ch + 1) * 128, :]))
    c.step("gp", gwv, writes=[pfx + "_wbig"], weight=16 * DC, tag="a3.wv")
    for t in range(NT):
        def pv(e, inc, mk, t=t):
            for hf in range(2):
                for ch in range(DC):
                    mk(e.matmul(psA[:, hf * 512:(hf + 1) * 512],
                                xnfm[:, ch * T + t * 128:ch * T + (t + 1) * 128],
                                wbig[:, ch * D + hf * 512:ch * D + (hf + 1) * 512],
                                start=(ch == 0), stop=(ch == DC - 1)))
        c.step("pe", pv, reads=[pfx + "_wbig", "l_xnfm"], writes=["psA"], tag=f"a3.v{t}")
        c.step("ve", lambda e, inc, mk, t=t: mk(e.tensor_tensor(
            vtok[:, t * D:(t + 1) * D], psA[:, 0:D], vb_bc[:, :], ALU.add)),
            reads=["psA", pfx + "_vbbc"], writes=[pfx + "_vtok"], tag=f"a3.vc{t}")

    for p in range(DC):
        def gqk(e, inc, mk, p=p):
            inc(e.dma_start(
                out=wqk[:, 0:D].rearrange("q (ch j) -> q ch j", j=128),
                in_=bass.AP(tensor=W["wq_t"].tensor, offset=p * 128,
                            ap=[[D, 128], [128 * D, DC], [1, 128]])))
            inc(e.dma_start(
                out=wqk[:, D:2 * D].rearrange("q (ch j) -> q ch j", j=128),
                in_=bass.AP(tensor=W["wk_t"].tensor, offset=p * 128,
                            ap=[[D, 128], [128 * D, DC], [1, 128]])))
        c.step("gp", gqk, writes=[pfx + "_wqk"], weight=32, tag=f"a3.qk{p}")

        for which, dst, dstn in (("q", qh, pfx + "_qh"), ("k", kh, pfx + "_kh")):
            woff = 0 if which == "q" else D

            def pp(e, inc, mk, woff=woff):
                for n4 in range(4):
                    for ch in range(DC):
                        mk(e.matmul(psA[:, n4 * 512:(n4 + 1) * 512],
                                    wqk[:, woff + ch * 128:woff + (ch + 1) * 128],
                                    xnfm[:, ch * T + n4 * 512:ch * T + (n4 + 1) * 512],
                                    start=(ch == 0), stop=(ch == DC - 1)))
            c.step("pe", pp, reads=[pfx + "_wqk", "l_xnfm"], writes=["psA"],
                   tag=f"a3.{which}p{p}")
            if which == "q":
                c.step("ve", lambda e, inc, mk, p=p: mk(e.tensor_scalar(
                    qh[:, :], psA[:, :], bqc[:, p:p + 1], None, ALU.add)),
                    reads=["psA", pfx + "_bqc"], writes=[pfx + "_qh"], tag=f"a3.qe{p}")
            else:
                c.step("ve", lambda e, inc, mk, p=p: mk(e.tensor_scalar(
                    kh[:, :], psA[:, :], bqc[:, 8 + p:9 + p], 0.125,
                    ALU.add, ALU.mult)),
                    reads=["psA", pfx + "_bqc"], writes=[pfx + "_kh"], tag=f"a3.ke{p}")

        for hh in range(2):
            h = 2 * p + hh
            base = hh * 64
            c.step("se", lambda e, inc, mk: mk(e.activation(
                out=sqq[:, :], in_=kh[:, :], func=AF.Square)),
                reads=[pfx + "_kh"], writes=[pfx + "_sqq"], tag=f"a3.k2{h}")

            def pkn(e, inc, mk, base=base):
                for k4 in range(4):
                    mk(e.matmul(psU[0:1, 512 + k4 * 128:512 + k4 * 128 + 128]
                                if False else psU[0:1, k4 * 512:(k4 + 1) * 512],
                                onescb[base:base + 64, 0:1],
                                sqq[base:base + 64, k4 * 512:(k4 + 1) * 512],
                                start=True, stop=True))
            c.step("pe", pkn, reads=[pfx + "_sqq", pfx + "_onescb"], writes=["psU"],
                   tag=f"a3.kn{h}")
            c.step("ve", lambda e, inc, mk: mk(e.reduce_max(
                scal[0:1, 6:7], psU[0:1, 0:2048], AX.X)),
                reads=["psU"], writes=["scal"], tag=f"a3.knr{h}")
            c.step("se", lambda e, inc, mk: mk(e.activation(
                out=scal[0:1, 6:7], in_=scal[0:1, 6:7], func=AF.Sqrt)),
                writes=["scal"], tag=f"a3.kns{h}")
            c.step("ve", lambda e, inc, mk: mk(e.tensor_add(
                scal[0:1, 7:8], scal[0:1, 6:7], scal[0:1, 4:5])),
                reads=["scal"], writes=["scal"], tag=f"a3.c1{h}")
            c.step("pe", lambda e, inc, mk: mk(e.matmul(
                psU[:, 600:601], ones1[0:1, :], scal[0:1, 7:8],
                start=True, stop=True)),
                reads=["scal", "c_ones128"], writes=["psU"], tag=f"a3.c1r{h}")
            c.step("ve", lambda e, inc, mk: mk(e.tensor_copy(
                bndc[:, 1:2], psU[:, 600:601])),
                reads=["psU"], writes=[pfx + "_bndc"], tag=f"a3.c1c{h}")
            c.step("se", lambda e, inc, mk: mk(e.activation(
                out=sqq[:, :], in_=qh[:, :], func=AF.Square)),
                reads=[pfx + "_qh"], writes=[pfx + "_sqq"], tag=f"a3.q2{h}")

            for lt in range(NT):
                L0 = lt * 128
                buf = lt % 2
                qoff = (lt % 2) * qrbuf.shape[1]
                aoff = (hh + 2 * p) % 2 * abuf.shape[1]

                c.step("pe", lambda e, inc, mk, base=base, L0=L0: mk(e.matmul(
                    psU[:, 640:641], sqq[base:base + 64, L0:L0 + 128],
                    onescb[base:base + 64, 0:1], start=True, stop=True)),
                    reads=[pfx + "_sqq", pfx + "_onescb"], writes=["psU"], tag=f"a3.b1.{h}.{lt}")
                c.step("se", lambda e, inc, mk: mk(e.activation(
                    out=bndc[:, 2:3], in_=psU[:, 640:641], func=AF.Sqrt)),
                    reads=["psU"], writes=[pfx + "_bndc"], tag=f"a3.b2.{h}.{lt}")
                c.step("ve", lambda e, inc, mk: mk(e.scalar_tensor_tensor(
                    bndc[:, 0:1], bndc[:, 2:3], -1.0, bndc[:, 1:2],
                    ALU.mult, ALU.mult)),
                    writes=[pfx + "_bndc"], tag=f"a3.b3.{h}.{lt}")

                def pqr(e, inc, mk, base=base, L0=L0):
                    cbase = 1920 - L0
                    for k4 in range(4):
                        mk(e.matmul(psA[:, k4 * 512:(k4 + 1) * 512],
                                    qh[base:base + 64, L0:L0 + 128],
                                    rkp[base:base + 64,
                                        cbase + k4 * 512:cbase + (k4 + 1) * 512],
                                    start=True, stop=True))
                    mk(e.matmul(psU[:, 0:128],
                                qh[base:base + 64, L0:L0 + 128],
                                rkp[base:base + 64, cbase + 2048:cbase + 2176],
                                start=True, stop=True))
                c.step("pe", pqr, reads=[pfx + "_qh", pfx + "_rkp"], writes=["psA", "psU"],
                       tag=f"a3.qr.{h}.{lt}")
                def vqr(e, inc, mk):
                    if bscale is None:
                        e.tensor_copy(qrw[:, 0:2048], psA[:, 0:2048])
                        mk(e.tensor_copy(qrw[:, 2048:2176], psU[:, 0:128]))
                    else:
                        e.tensor_scalar(qrw[:, 0:2048], psA[:, 0:2048], bscale,
                                        None, ALU.mult)
                        mk(e.tensor_scalar(qrw[:, 2048:2176], psU[:, 0:128], bscale,
                                           None, ALU.mult))
                c.step("ve", vqr, reads=["psA", "psU"], writes=[pfx + "_qrw"],
                       tag=f"a3.qc.{h}.{lt}")
                c.step("gp", lambda e, inc, mk, qoff=qoff: inc(e.dma_start(
                    out=bass.AP(tensor=qrbuf, offset=qoff,
                                ap=[[P2 + 1, 128], [1, 2176]]),
                    in_=qrw[:, 0:2176])),
                    reads=[pfx + "_qrw"], writes=[r_qr[buf]], weight=16,
                    tag=f"a3.qw.{h}.{lt}")

                def pst(e, inc, mk, base=base, L0=L0):
                    for k4 in range(4):
                        mk(e.matmul(psA[:, k4 * 512:(k4 + 1) * 512],
                                    qh[base:base + 64, L0:L0 + 128],
                                    kh[base:base + 64, k4 * 512:(k4 + 1) * 512],
                                    start=True, stop=True))
                c.step("pe", pst, reads=[pfx + "_qh", pfx + "_kh"], writes=["psA"],
                       tag=f"a3.s.{h}.{lt}")
                c.step("sy", lambda e, inc, mk, qoff=qoff: inc(e.dma_start(
                    out=bsb[:, :],
                    in_=bass.AP(tensor=qrbuf, offset=qoff + 127,
                                ap=[[P2, 128], [1, 2048]]))),
                    reads=[r_qr[buf]], writes=[pfx + "_bsb"], weight=16,
                    tag=f"a3.br.{h}.{lt}")
                c.step("ve", lambda e, inc, mk: mk(e.scalar_tensor_tensor(
                    atf[:, :], psA[:, :], 1.0, bsb[:, :], ALU.mult, ALU.add)),
                    reads=["psA", pfx + "_bsb"], writes=[pfx + "_atf"], tag=f"a3.sb.{h}.{lt}")
                c.step("ve", lambda e, inc, mk: mk(e.reduce_max(
                    bndc[:, 0:1], atf[:, :], AX.X, negate=True)),
                    reads=[pfx + "_atf"], writes=[pfx + "_bndc"], tag=f"a3.mx.{h}.{lt}")
                def sex(e, inc, mk):
                    for k4 in range(4):
                        mk(e.activation(out=at[:, k4 * 512:(k4 + 1) * 512],
                                        in_=atf[:, k4 * 512:(k4 + 1) * 512],
                                        func=AF.Exp, bias=bndc[:, 0:1]))
                c.step("se", sex, reads=[pfx + "_atf", pfx + "_bndc"], writes=[pfx + "_at"],
                       tag=f"a3.e.{h}.{lt}")
                c.step("gp", lambda e, inc, mk, aoff=aoff, L0=L0: inc(e.dma_start(
                    out=bass.AP(tensor=abuf,
                                offset=aoff + B0 + L0 * (PA + 1),
                                ap=[[PA + 1, 128], [1, 2048]]),
                    in_=at[:, :])),
                    reads=[pfx + "_at"], writes=[r_ab[(hh + 2 * p) % 2]], weight=16,
                    tag=f"a3.aw.{h}.{lt}")

            abn = (hh + 2 * p) % 2
            aoff = abn * abuf.shape[1]
            def pclr(e, inc, mk):
                for k4 in range(4):
                    mk(e.matmul(psU[0:65, k4 * 512:(k4 + 1) * 512], one65[0:1, :],
                                zc512[0:1, :], start=True, stop=False))
            c.step("pe", pclr, reads=[pfx + "_one65", pfx + "_zc512"], writes=["psU"],
                   tag=f"a3.clr{h}")
            for jc in ([] if (L3_NO_RELV or not relv) else range(32)):
                c.step("sy", lambda e, inc, mk, aoff=aoff, jc=jc: inc(
                    e.dma_start_transpose(
                        wjt[:, :],
                        bass.AP(tensor=abuf,
                                offset=aoff + B0 - 2048 + jc * 128,
                                ap=[[PA + 2, 2048], [1, 128]]))),
                    reads=[r_ab[abn]], writes=[pfx + "_wjt"], weight=16,
                    tag=f"a3.wtr.{h}.{jc}")

                def prv(e, inc, mk, jc=jc):
                    for k4 in range(4):
                        mk(e.matmul(psU[0:64, k4 * 512:(k4 + 1) * 512],
                                    rvp[:, jc * 64:(jc + 1) * 64],
                                    wjt[:, k4 * 512:(k4 + 1) * 512],
                                    start=False, stop=False))
                c.step("pe", prv, reads=[pfx + "_rvp", pfx + "_wjt"], writes=["psU"],
                       tag=f"a3.rv.{h}.{jc}")
            for rt in range(NT):
                c.step("sy", lambda e, inc, mk, aoff=aoff, rt=rt: inc(
                    e.dma_start_transpose(
                        att[:, :],
                        bass.AP(tensor=abuf, offset=aoff + B0 + rt * 128,
                                ap=[[PA + 1, 2048], [1, 128]]))),
                    reads=[r_ab[abn]], writes=[pfx + "_att"], weight=16,
                    tag=f"a3.atr.{h}.{rt}")
                c.step("ve", lambda e, inc, mk, rt=rt, h=h: mk(e.tensor_copy(
                    vh[:, 0:64], vtok[:, rt * D + 64 * h:rt * D + 64 * h + 64])),
                    reads=[pfx + "_vtok"], writes=[pfx + "_vh"], tag=f"a3.vh.{h}.{rt}")

                def pav(e, inc, mk, rt=rt):
                    for k4 in range(4):
                        mk(e.matmul(psU[0:65, k4 * 512:(k4 + 1) * 512], vh[:, 0:65],
                                    att[:, k4 * 512:(k4 + 1) * 512],
                                    start=False, stop=(rt == NT - 1)))
                c.step("pe", pav, reads=[pfx + "_vh", pfx + "_att"], writes=["psU"],
                       tag=f"a3.av.{h}.{rt}")

            c.step("ve", lambda e, inc, mk: mk(e.tensor_copy(nsb[:, :], psU[0:64, :])),
                   reads=["psU"], writes=[pfx + "_nsb"], tag=f"a3.nm{h}")
            c.step("se", lambda e, inc, mk: mk(e.activation(
                out=rdr[0:1, :], in_=psU[64:65, :], func=AF.Ln)),
                reads=["psU"], writes=[pfx + "_rdr"], tag=f"a3.rd{h}")
            c.step("se", lambda e, inc, mk: mk(e.activation(
                out=rdr[0:1, :], in_=rdr[0:1, :], func=AF.Exp, scale=-1.0)),
                writes=[pfx + "_rdr"], tag=f"a3.rd2{h}")

            def pbc2(e, inc, mk):
                for k4 in range(4):
                    mk(e.matmul(psU[0:64, k4 * 512:(k4 + 1) * 512], ones1[0:1, 0:64],
                                rdr[0:1, k4 * 512:(k4 + 1) * 512], start=True,
                                stop=True))
            c.step("pe", pbc2, reads=[pfx + "_rdr", "c_ones128"], writes=["psU"],
                   tag=f"a3.rb{h}")
            c.step("ve", lambda e, inc, mk, p=p, base=base: mk(e.tensor_tensor(
                mrg[base:base + 64, p * T:(p + 1) * T], nsb[:, :], psU[0:64, :],
                ALU.mult)),
                reads=[pfx + "_nsb", "psU"], writes=[pfx + "_mrg"], tag=f"a3.mg{h}")

    def gwo(e, inc, mk):
        for ch in range(DC):
            inc(e.dma_start(out=wbig[:, ch * D:(ch + 1) * D],
                            in_=W["wo_t"][ch * 128:(ch + 1) * 128, :]))
    c.step("gp", gwo, writes=[pfx + "_wbig"], weight=16 * DC, tag="a3.wo")
    for ech in range(DC):
        def po(e, inc, mk, ech=ech):
            for n4 in range(4):
                for ch in range(DC):
                    mk(e.matmul(psA[:, n4 * 512:(n4 + 1) * 512],
                                wbig[:, ch * D + ech * 128:ch * D + (ech + 1) * 128],
                                mrg[:, ch * T + n4 * 512:ch * T + (n4 + 1) * 512],
                                start=(ch == 0), stop=(ch == DC - 1)))
        c.step("pe", po, reads=[pfx + "_wbig", pfx + "_mrg"], writes=["psA"], tag=f"a3.o{ech}")
        c.step("ve", lambda e, inc, mk, ech=ech: mk(e.tensor_scalar(
            sfmt[:, :], psA[:, :], boc[:, ech:ech + 1], None, ALU.add)),
            reads=["psA", pfx + "_boc"], writes=[pfx + "_sfmt"], tag=f"a3.oe{ech}")
        c.step("gp", lambda e, inc, mk, ech=ech: inc(e.dma_start(
            out=s_fm[ech * 128:(ech + 1) * 128, :], in_=sfmt[:, :])),
            reads=[pfx + "_sfmt"], writes=[s_fm_res], weight=16, tag=f"a3.os{ech}")


def build_layer3(nc, dbg=False):
    c = Ctx(nc)
    x_in = nc.dram_tensor("x_in", [T, D], F32, kind="ExternalInput").ap()
    x_out = nc.dram_tensor("x_out", [T, D], F32, kind="ExternalOutput").ap()
    wnames = {
        "wq_t": [D, D], "wk_t": [D, D], "wv_t": [D, D], "wo_t": [D, D],
        "bq": [D], "bk": [D], "bv": [D], "bo": [D],
        "relk_pad": [64, 4096], "relv_pad": [4096, 64],
        "clip_x": [1], "clip_down": [1],
        "w_gate": [2816, D], "w_up": [2816, D], "w_down": [D, 2816],
        "w_gate_t": [D, 2816], "w_up_t": [D, 2816], "w_down_t": [2816, D],
    }
    W = {k: nc.dram_tensor("w_" + k, shp, F32, kind="ExternalInput").ap()
         for k, shp in wnames.items()}
    s_fm = nc.dram_tensor("s_fm", [D, T], F32,
                          kind="ExternalOutput" if dbg else "Internal")
    declare_common(c)
    r_sfm = c.newres("r_sfm")
    r_xout = c.newres("r_xout")

    def _ldclips(e, inc, mk):
        inc(e.dma_start(out=c.t_tiny[0:1, 0:1], in_=W["clip_x"][None, :]))
        inc(e.dma_start(out=c.t_tiny[0:1, 1:2], in_=W["clip_down"][None, :]))
    c.step("sy", _ldclips, writes=["tiny"], weight=32, tag="clips")
    c.step("se", lambda e, inc, mk: mk(e.activation(
        out=c.t_tiny[0:1, 2:4], in_=c.t_tiny[0:1, 0:2], func=AF.Sigmoid)),
        reads=["tiny"], writes=["tiny"], tag="sig")

    with scoped(c):
        xnfm = c.sb("l_xnfm", [128, DC * T], BF16)
        emit_ln_pass(c, x_in, xnfm, "l_xnfm")
        emit_allreduce_max(c, "l3_ccx", 1)
        emit_quant_scale(c, 2, 1, 0)
        emit_replicate(c, 0, 0)
        emit_replicate(c, 1, 1)
        emit_attn_l3(c, W, xnfm, s_fm.ap(), r_sfm)
        if dbg:
            dbg3 = nc.dram_tensor("dbg3", [128, 4108], F32, kind="ExternalOutput").ap()
            rd3 = c.newres("rd3")
            def _dd(e, inc, mk):
                inc(e.dma_start(out=dbg3[:, 0:2048], in_=c.res_dump_at[:, :]))
                inc(e.dma_start(out=dbg3[:, 2048:2052], in_=c.res_dump_bndc[:, :]))
                inc(e.dma_start(out=dbg3[0:1, 2052:2068], in_=c.t_scal[0:1, :]))
                inc(e.dma_start(out=dbg3[0:64, 2068:4116 - 8][:, 0:2040],
                                in_=c.res_dump_nsb[0:64, 0:2040]))
            c.step("gp", _dd, reads=["a3_at", "a3_bndc", "scal", "a3_nsb"],
                   writes=[rd3], weight=64, tag="dbg3")
            dbg4 = nc.dram_tensor("dbg4", [1, 2048], F32, kind="ExternalOutput").ap()
            rd4 = c.newres("rd4")
            c.step("ve", lambda e, inc, mk: mk(e.tensor_copy(
                c.res_dump_qrw[0:1, 0:2048], c.t_psU[64:65, :])),
                reads=["psU"], writes=["a3_qrw"], tag="dbg4c")
            c.step("gp", lambda e, inc, mk: inc(e.dma_start(
                out=dbg4[0:1, :], in_=c.res_dump_qrw[0:1, 0:2048])),
                reads=["a3_qrw"], writes=[rd4], weight=16, tag="dbg4")

    xq_t, pacc_t = emit_swiglu_ffn(c, "f3", W, x_in, s_fm.ap(), r_sfm)
    emit_residual_combine(c, x_in, x_out, r_xout, pacc_t, "f3pacc")
    c.step("gp", lambda e, inc, mk: None, reads=[r_xout], tag="final")
    return c
def emit_moe_ffn(c, W, x_in, s_fm, s_fm_res, stage=99, dbg=False, sub=99):
    nc = c.nc
    psA, psU = c.t_psA, c.t_psU
    E, CAP, SLAB = 8, 4096, 640
    H = 4096
    NH = H // 128

    glog_in = nc.dram_tensor("m_login", [T, E], F32, kind="Internal")
    glog_out = nc.dram_tensor("m_logout", [NCORES * T, E], F32, kind="Internal")
    coefbuf = nc.dram_tensor("m_coef", [128, 1024], F32,
                             kind="ExternalOutput" if dbg and stage >= 1 else "Internal")
    disp = nc.dram_tensor("m_disp", [E * SLAB + 1, D], BF16,
                          kind="ExternalOutput" if dbg and stage >= 2 else "Internal")
    eobuf = nc.dram_tensor("m_eo", [E * SLAB + 1, D], F32,
                           kind="ExternalOutput" if dbg and stage >= 3 else "Internal")
    r_login, r_logout, r_coef, r_disp, r_eo = (
        c.newres(n) for n in ("m_login", "m_logout", "m_coef",
                              "m_disp", "m_eo"))

    pacc = c.sb("m_pacc", [128, NT * D])
    if stage < 4 or sub < 4:
        def pz(e, inc, mk):
            for t in range(NT):
                ins = e.memset(pacc[:, t * D:(t + 1) * D], 0.0)
            mk(ins)
        c.step("ve", pz, writes=["m_pacc"], tag="mz")

    with scoped(c):
        gwT = c.sb("m_gwT", [128, 64])
        xfm32 = c.sb("m_xfm", [128, D])
        lgt = c.sb("m_lgt", [128, E])
        c.step("sy", lambda e, inc, mk: inc(e.dma_start(
            out=gwT[:, :].rearrange("p (ch j) -> p ch j", j=8),
            in_=bass.AP(tensor=W["gate_wt"].tensor, offset=0,
                        ap=[[E, 128], [128 * E, DC], [1, E]]))),
            writes=["m_gwT"], weight=16, tag="mg.w")
        for t in range(NT):
            c.step("sy", lambda e, inc, mk, t=t: inc(e.dma_start(
                out=c.t_xt[:, :], in_=x_in[t * 128:(t + 1) * 128, :])),
                writes=["xt"], weight=16, tag=f"mg.l{t}")
            c.step("ve", lambda e, inc, mk, t=t: mk(e.tensor_scalar(
                c.t_xn32[:, :], c.t_xt[:, :], c.t_mu[:, t:t + 1], c.t_rs[:, t:t + 1],
                ALU.subtract, ALU.mult)),
                reads=["xt", "stat_mu", "stat_rs"], writes=["xn32"], tag="mg.n")

            def ptr(e, inc, mk):
                for ch in range(DC):
                    mk(e.transpose(psA[:, ch * 128:(ch + 1) * 128],
                                   c.t_xn32[:, ch * 128:(ch + 1) * 128],
                                   c.cst["identf"][:, :]))
            c.step("pe", ptr, reads=["xn32", "c_identf"], writes=["psA"], tag="mg.t")
            c.step("ve", lambda e, inc, mk: mk(e.tensor_copy(
                xfm32[:, :], psA[:, 0:D])),
                reads=["psA"], writes=["m_xfm"], tag="mg.c")

            def pl(e, inc, mk):
                for ch in range(DC):
                    mk(e.matmul(psU[:, 0:E], xfm32[:, ch * 128:(ch + 1) * 128],
                                gwT[:, ch * 8:(ch + 1) * 8],
                                start=(ch == 0), stop=(ch == DC - 1)))
            c.step("pe", pl, reads=["m_xfm", "m_gwT"], writes=["psU"], tag="mg.m")
            c.step("ve", lambda e, inc, mk: mk(e.tensor_copy(lgt[:, :], psU[:, 0:E])),
                   reads=["psU"], writes=["m_lgt"], tag="mg.e")
            c.step("gp", lambda e, inc, mk, t=t: inc(e.dma_start(
                out=glog_in.ap()[t * 128:(t + 1) * 128, :], in_=lgt[:, :])),
                reads=["m_lgt"], writes=[r_login], weight=16, tag=f"mg.s{t}")
        c.step("gp", lambda e, inc, mk: inc(e.collective_compute(
            "AllGather", ALU.bypass, replica_groups=[list(range(NCORES))],
            ins=[glog_in.ap().opt()], outs=[glog_out.ap().opt()]), 1),
            reads=[r_login], writes=[r_logout], weight=1, tag="mg.cc")

    with scoped(c):
        lg = c.sb("m_lg", [128, 1024])
        sc1 = c.sb("m_sc1", [128, 1024])
        sc2 = c.sb("m_sc2", [128, 1024])
        mk1 = c.sb("m_mk1", [128, 1024])
        mk2 = c.sb("m_mk2", [128, 1024])
        gx = c.sb("m_gx", [128, 128])
        gy = c.sb("m_gy", [128, 128])
        g1 = c.sb("m_g1", [128, 128])
        g2 = c.sb("m_g2", [128, 128])
        zsc = c.sb("m_zsc", [128, 128])

        def gAP(t, g0=0):
            return t[:, :].rearrange("p (g e) -> p g e", e=8)

        def gbc(t, col=0):
            return bass.AP(tensor=t, offset=col, ap=[[128, 128], [1, 128], [0, 8]])

        def ebc(t):
            return bass.AP(tensor=t, offset=0, ap=[[8, 128], [0, 128], [1, 8]])

        c.step("sy", lambda e, inc, mk: inc(e.dma_start(
            out=lg[:, :], in_=glog_out.ap().rearrange("(p x) e -> p (x e)", p=128))),
            reads=[r_logout], writes=["m_lg"], weight=16, tag="mr.l")
        c.step("ve", lambda e, inc, mk: mk(e.memset(zsc[:, :], 0.0)),
               writes=["m_zsc"], tag="mr.z")
        c.step("ve", lambda e, inc, mk: mk(e.tensor_reduce(
            gx[:, :], gAP(lg), AX.X, ALU.max)),
            reads=["m_lg"], writes=["m_gx"], tag="mr.mx")
        c.step("ve", lambda e, inc, mk: mk(e.tensor_tensor(
            sc1[:, :].rearrange("p (g e) -> p g e", e=8), gAP(lg), gbc(gx),
            ALU.subtract)),
            reads=["m_lg", "m_gx"], writes=["m_sc1"], tag="mr.sub")
        c.step("se", lambda e, inc, mk: mk(e.activation(
            out=sc1[:, :], in_=sc1[:, :], func=AF.Exp)),
            writes=["m_sc1"], tag="mr.exp")
        c.step("ve", lambda e, inc, mk: mk(e.tensor_reduce(
            gy[:, :], gAP(sc1), AX.X, ALU.add)),
            reads=["m_sc1"], writes=["m_gy"], tag="mr.sum")
        c.step("ve", lambda e, inc, mk: mk(e.reciprocal(gy[:, :], gy[:, :])),
               writes=["m_gy"], tag="mr.rc")
        c.step("ve", lambda e, inc, mk: mk(e.tensor_tensor(
            sc1[:, :].rearrange("p (g e) -> p g e", e=8), gAP(sc1), gbc(gy),
            ALU.mult)),
            reads=["m_gy"], writes=["m_sc1"], tag="mr.p")
        c.step("ve", lambda e, inc, mk: mk(e.tensor_reduce(
            gx[:, :], gAP(sc1), AX.X, ALU.max)),
            reads=["m_sc1"], writes=["m_gx"], tag="mr.t1")
        c.step("ve", lambda e, inc, mk: mk(e.tensor_tensor(
            mk1[:, :].rearrange("p (g e) -> p g e", e=8), gAP(sc1), gbc(gx),
            ALU.is_ge)),
            reads=["m_sc1", "m_gx"], writes=["m_mk1"], tag="mr.m1")
        c.step("ve", lambda e, inc, mk: mk(e.scalar_tensor_tensor(
            sc2[:, :], mk1[:, :], -4.0, sc1[:, :], ALU.mult, ALU.add)),
            reads=["m_mk1", "m_sc1"], writes=["m_sc2"], tag="mr.s2")
        c.step("ve", lambda e, inc, mk: mk(e.tensor_reduce(
            gy[:, :], gAP(sc2), AX.X, ALU.max)),
            reads=["m_sc2"], writes=["m_gy"], tag="mr.t2")
        c.step("ve", lambda e, inc, mk: mk(e.tensor_tensor(
            mk2[:, :].rearrange("p (g e) -> p g e", e=8), gAP(sc2), gbc(gy),
            ALU.is_ge)),
            reads=["m_sc2", "m_gy"], writes=["m_mk2"], tag="mr.m2")
        c.step("ve", lambda e, inc, mk: mk(e.tensor_add(g1[:, :], gx[:, :], gy[:, :])),
               reads=["m_gx", "m_gy"], writes=["m_g1"], tag="mr.gs")
        c.step("ve", lambda e, inc, mk: mk(e.reciprocal(g1[:, :], g1[:, :])),
               writes=["m_g1"], tag="mr.gr")
        c.step("ve", lambda e, inc, mk: mk(e.tensor_mul(g2[:, :], gy[:, :], g1[:, :])),
               reads=["m_gy", "m_g1"], writes=["m_g2"], tag="mr.g2")
        c.step("ve", lambda e, inc, mk: mk(e.tensor_mul(g1[:, :], gx[:, :], g1[:, :])),
               reads=["m_gx"], writes=["m_g1"], tag="mr.g1")

        ch1 = c.sb("m_ch1", [128, 1024])
        ch2 = c.sb("m_ch2", [128, 1024])
        c.step("ve", lambda e, inc, mk: mk(e.tensor_copy(ch1[:, :], mk1[:, :])),
               reads=["m_mk1"], writes=["m_ch1"], tag="mr.k1")
        c.step("ve", lambda e, inc, mk: mk(e.tensor_copy(ch2[:, :], mk2[:, :])),
               reads=["m_mk2"], writes=["m_ch2"], tag="mr.k2")

        for mkt, mkn in ((mk1, "m_mk1"), (mk2, "m_mk2")):
            for e8 in range(8):
                c.step("ve", lambda e, inc, mk, e8=e8, mkt=mkt: mk(e.tensor_tensor_scan(
                    bass.AP(tensor=mkt, offset=e8, ap=[[1024, 128], [8, 128]]),
                    bass.AP(tensor=mkt, offset=e8, ap=[[1024, 128], [8, 128]]),
                    zsc[:, :], 0.0, ALU.add, ALU.add)),
                    reads=["m_zsc"], writes=[mkn], tag=f"mr.cs{e8}")

        sm8a = c.sb("m_sm8a", [128, 8])
        sm8c = c.sb("m_sm8c", [128, 8])
        c1row = c.sb("m_c1row", [1, 8])
        gR01 = c.sb("m_gR01", [1, 8])
        gOC1 = c.sb("m_gOC1", [1, 8])
        gR02 = c.sb("m_gR02", [1, 8])
        for mkt, mkn, sm, smn in ((mk1, "m_mk1", sm8a, "m_sm8a"),
                                  (mk2, "m_mk2", sm8c, "m_sm8c")):
            c.step("pe", lambda e, inc, mk, mkt=mkt: mk(e.matmul(
                psA[:, 0:8], c.cst["upper_strict"][:, :], mkt[:, 1016:1024],
                start=True, stop=True)),
                reads=[mkn, "c_upper_strict"], writes=["psA"], tag="mr.pf")
            c.step("ve", lambda e, inc, mk, sm=sm: mk(e.tensor_copy(
                sm[:, :], psA[:, 0:8])),
                reads=["psA"], writes=[smn], tag="mr.pc")
        c.step("pe", lambda e, inc, mk: mk(e.matmul(
            psA[0:1, 512:520], c.t_onesf[:, 0:1], mk1[:, 1016:1024],
            start=True, stop=True)),
            reads=["m_mk1", "onesf"], writes=["psA"], tag="mr.tt")
        c.step("ve", lambda e, inc, mk: mk(e.tensor_copy(
            c1row[0:1, :], psA[0:1, 512:520])),
            reads=["psA"], writes=["m_c1row"], tag="mr.tc")

        def selmm(e, inc, mk):
            mk(e.matmul(psA[0:1, 24:32], c.t_selv[:, 0:1], sm8a[:, :],
                        start=True, stop=True))
            mk(e.matmul(psA[0:1, 536:544], c.t_selv[:, 1:2], mk1[:, 1016:1024],
                        start=True, stop=True))
            mk(e.matmul(psA[0:1, 1048:1056], c.t_selv[:, 0:1], sm8c[:, :],
                        start=True, stop=True))
        c.step("pe", selmm, reads=["selv", "m_sm8a", "m_mk1", "m_sm8c"],
               writes=["psA"], tag="mr.sel")

        def selcp(e, inc, mk):
            e.tensor_copy(gR01[0:1, :], psA[0:1, 24:32])
            e.tensor_copy(gOC1[0:1, :], psA[0:1, 536:544])
            mk(e.tensor_copy(gR02[0:1, :], psA[0:1, 1048:1056]))
        c.step("ve", selcp, reads=["psA"],
               writes=["m_gR01", "m_gOC1", "m_gR02"], tag="mr.sc")

        bR01 = c.sb("m_bR01", [128, 8])
        bOC1 = c.sb("m_bOC1", [128, 8])
        bC1T = c.sb("m_bC1T", [128, 8])
        bR02 = c.sb("m_bR02", [128, 8])
        bER = c.sb("m_bER", [128, 8])

        def bc18(src_row, dst, dstn, reads):
            c.step("pe", lambda e, inc, mk: mk(e.matmul(
                psA[:, 16:24], c.cst["ones128"][0:1, :], src_row,
                start=True, stop=True)),
                reads=reads + ["c_ones128"], writes=["psA"], tag="mr.b")
            c.step("ve", lambda e, inc, mk: mk(e.tensor_copy(dst[:, :], psA[:, 16:24])),
                   reads=["psA"], writes=[dstn], tag="mr.bc")
        bc18(gR01[0:1, :], bR01, "m_bR01", ["m_gR01"])
        bc18(gR02[0:1, :], bR02, "m_bR02", ["m_gR02"])
        bc18(c.t_eramp[0:1, :], bER, "m_bER", ["eramp"])
        bc18(c1row[0:1, :], bC1T, "m_bC1T", ["m_c1row"])
        bc18(gOC1[0:1, :], bOC1, "m_bOC1", ["m_gOC1"])

        r1 = c.sb("m_r1", [128, 1024])
        r2 = c.sb("m_r2", [128, 1024])
        vt1 = c.sb("m_vt1", [128, 128])
        vt2 = c.sb("m_vt2", [128, 128])
        sl1 = c.sb("m_sl1", [128, 128])
        sl2 = c.sb("m_sl2", [128, 128])
        c1v = c.sb("m_c1v", [128, 128])
        c2v = c.sb("m_c2v", [128, 128])
        tmpg = c.sb("m_tmpg", [128, 1024])

        def rank_pass(cum, cumn, ch, chn, smp, smpn, extra, r, rn, vt, vtn,
                      baseb, basen, sl, sln, gsel, gseln, cv, cvn):
            c.step("ve", lambda e, inc, mk: mk(e.tensor_tensor(
                r[:, :].rearrange("p (g e) -> p g e", e=8),
                cum[:, :].rearrange("p (g e) -> p g e", e=8), ebc(smp), ALU.add)),
                reads=[cumn, smpn], writes=[rn], tag="rk.a")
            if extra is not None:
                c.step("ve", lambda e, inc, mk: mk(e.tensor_tensor(
                    r[:, :].rearrange("p (g e) -> p g e", e=8),
                    r[:, :].rearrange("p (g e) -> p g e", e=8), ebc(extra[0]),
                    ALU.add)),
                    reads=[extra[1]], writes=[rn], tag="rk.x")
            c.step("ve", lambda e, inc, mk: mk(e.tensor_sub(r[:, :], r[:, :], ch[:, :])),
                   reads=[chn], writes=[rn], tag="rk.s")
            c.step("ve", lambda e, inc, mk: mk(e.tensor_single_scalar(
                tmpg[:, :], r[:, :], 4096.0, ALU.is_lt)),
                reads=[rn], writes=["m_tmpg"], tag="rk.v")
            c.step("ve", lambda e, inc, mk: mk(e.tensor_mul(
                tmpg[:, :], tmpg[:, :], ch[:, :])),
                reads=[chn], writes=["m_tmpg"], tag="rk.vm")
            c.step("ve", lambda e, inc, mk: mk(e.tensor_reduce(
                vt[:, :], tmpg[:, :].rearrange("p (g e) -> p g e", e=8), AX.X,
                ALU.add)),
                reads=["m_tmpg"], writes=[vtn], tag="rk.vr")
            c.step("ve", lambda e, inc, mk: mk(e.tensor_tensor(
                tmpg[:, :].rearrange("p (g e) -> p g e", e=8),
                r[:, :].rearrange("p (g e) -> p g e", e=8), ebc(baseb), ALU.subtract)),
                reads=[rn, basen], writes=["m_tmpg"], tag="rk.l")
            c.step("ve", lambda e, inc, mk: mk(e.tensor_mul(
                tmpg[:, :], tmpg[:, :], ch[:, :])),
                reads=[chn], writes=["m_tmpg"], tag="rk.lm")
            c.step("ve", lambda e, inc, mk: mk(e.tensor_reduce(
                sl[:, :], tmpg[:, :].rearrange("p (g e) -> p g e", e=8), AX.X,
                ALU.add)),
                reads=["m_tmpg"], writes=[sln], tag="rk.lr")
            c.step("ve", lambda e, inc, mk: mk(e.tensor_mul(
                cv[:, :], gsel[:, :], vt[:, :])),
                reads=[gseln, vtn], writes=[cvn], tag="rk.cv")
            c.step("ve", lambda e, inc, mk: mk(e.tensor_mul(
                sl[:, :], sl[:, :], vt[:, :])),
                reads=[vtn], writes=[sln], tag="rk.sv")
            c.step("ve", lambda e, inc, mk: mk(e.tensor_scalar(
                tmpg[:, 0:128], vt[:, :], -5120.0, 5120.0, ALU.mult, ALU.add)),
                reads=[vtn], writes=["m_tmpg"], tag="rk.bg")
            c.step("ve", lambda e, inc, mk: mk(e.tensor_add(
                sl[:, :], sl[:, :], tmpg[:, 0:128])),
                reads=["m_tmpg"], writes=[sln], tag="rk.sf")
            c.step("ve", lambda e, inc, mk: mk(e.tensor_scalar(
                sl[:, :], sl[:, :], 0.0, 5120.0, ALU.max, ALU.min)),
                writes=[sln], tag="rk.cl")

        bb1 = c.sb("m_bb1", [128, 8])
        c.step("ve", lambda e, inc, mk: mk(e.tensor_sub(bb1[:, :], bR01[:, :], bER[:, :])),
               reads=["m_bR01", "m_bER"], writes=["m_bb1"], tag="mr.bb1")
        rank_pass(mk1, "m_mk1", ch1, "m_ch1", sm8a, "m_sm8a", None,
                  r1, "m_r1", vt1, "m_vt1", bb1, "m_bb1", sl1, "m_sl1",
                  g1, "m_g1", c1v, "m_c1v")
        bb2 = c.sb("m_bb2", [128, 8])
        c.step("ve", lambda e, inc, mk: mk(e.tensor_add(bb2[:, :], bC1T[:, :], bR02[:, :])),
               reads=["m_bC1T", "m_bR02"], writes=["m_bb2"], tag="mr.bb2a")
        c.step("ve", lambda e, inc, mk: mk(e.tensor_sub(bb2[:, :], bb2[:, :], bER[:, :])),
               reads=["m_bER"], writes=["m_bb2"], tag="mr.bb2b")
        c.step("ve", lambda e, inc, mk: mk(e.tensor_sub(bb2[:, :], bb2[:, :], bOC1[:, :])),
               reads=["m_bOC1"], writes=["m_bb2"], tag="mr.bb2c")
        rank_pass(mk2, "m_mk2", ch2, "m_ch2", sm8c, "m_sm8c", (bC1T, "m_bC1T"),
                  r2, "m_r2", vt2, "m_vt2", bb2, "m_bb2", sl2, "m_sl2",
                  g2, "m_g2", c2v, "m_c2v")

        coefT = c.sb("m_coefT", [128, 1024])
        c.step("ve", lambda e, inc, mk: mk(e.memset(coefT[:, :], 0.0)),
               writes=["m_coefT"], tag="mr.cz")
        for src, srcn, off in ((c1v, "m_c1v", 0), (c2v, "m_c2v", 1),
                               (sl1, "m_sl1", 2), (sl2, "m_sl2", 3)):
            c.step("ve", lambda e, inc, mk, src=src, off=off: mk(e.tensor_copy(
                bass.AP(tensor=coefT, offset=off, ap=[[1024, 128], [8, 128]]),
                src[:, :])),
                reads=[srcn], writes=["m_coefT"], tag="mr.ca")
        c.step("gp", lambda e, inc, mk: inc(e.dma_start(
            out=coefbuf.ap()[:, :], in_=coefT[:, :])),
            reads=["m_coefT"], writes=[r_coef], weight=16, tag="mr.cw")

    if stage < 2:
        return pacc
    with scoped(c):
        xnb = c.sb("m_xnb", [128, D], BF16)
        oc = c.sb("m_oc", [128, 8])
        oci = c.sb("m_oci", [128, 2], I32)
        zb = c.sb("m_zb", [128, D], BF16)
        zf = c.sb("m_zf", [1, D])
        c.step("ve", lambda e, inc, mk: mk(e.memset(zb[:, :], 0.0)),
               writes=["m_zb"], tag="md.z")
        c.step("ve", lambda e, inc, mk: mk(e.memset(zf[:, :], 0.0)),
               writes=["m_zf"], tag="md.zf")

        def zdisp(e, inc, mk):
            for i in range(41):
                n = min(128, E * SLAB + 1 - i * 128)
                inc(e.dma_start(out=disp.ap()[i * 128:i * 128 + n, :],
                                in_=zb[0:n, :]))
            inc(e.dma_start(out=eobuf.ap()[E * SLAB:E * SLAB + 1, :],
                            in_=zf[0:1, :]))
        c.step("gp", zdisp, reads=["m_zb", "m_zf"], writes=[r_disp, r_eo],
               weight=16 * 42, tag="md.zd")

        for tt in range(NT):
            c.step("gp", lambda e, inc, mk, tt=tt: inc(e.indirect_dma_start(
                oc[:, :], None, coefbuf.ap().rearrange("p (g e) -> (p g) e", e=8),
                IndirectOffsetOnAxis(c.t_orows[:, tt:tt + 1], 0))),
                reads=[r_coef, "orows"], writes=["m_oc"], weight=16, tag=f"md.g{tt}")
            c.step("ve", lambda e, inc, mk: mk(e.tensor_copy(
                oci[:, :], oc[:, 2:4])),
                reads=["m_oc"], writes=["m_oci"], tag="md.i")
            c.step("sy", lambda e, inc, mk, tt=tt: inc(e.dma_start(
                out=c.t_xt[:, :], in_=x_in[tt * 128:(tt + 1) * 128, :])),
                writes=["xt"], weight=16, tag=f"md.x{tt}")
            c.step("ve", lambda e, inc, mk, tt=tt: mk(e.tensor_scalar(
                xnb[:, :], c.t_xt[:, :], c.t_mu[:, tt:tt + 1], c.t_rs[:, tt:tt + 1],
                ALU.subtract, ALU.mult)),
                reads=["xt", "stat_mu", "stat_rs"], writes=["m_xnb"], tag="md.n")
            for j in range(2):
                c.step("gp", lambda e, inc, mk, j=j: inc(e.indirect_dma_start(
                    disp.ap(), IndirectOffsetOnAxis(oci[:, j:j + 1], 0),
                    xnb[:, :], None)),
                    reads=["m_oci", "m_xnb"], writes=[r_disp], weight=16,
                    tag=f"md.s{j}")

    if stage < 3 or SKIPEXP:
        if stage >= 4 and SKIPEXP:
            pass
        else:
            return pacc
    if not SKIPEXP:
      with scoped(c):
        dfm = c.sb("m_dfm", [128, DC * SLAB], BF16)
        hbe = c.sb("m_hbe", [128, NH * SLAB], BF16)
        w1sl = [c.sb(f"m_w1sl{i}", [128, DC * 128]) for i in range(2)]
        w1b = [c.sb(f"m_w1b{i}", [128, DC * 128], BF16) for i in range(2)]
        w2sl = [c.sb(f"m_w2sl{i}", [128, 128]) for i in range(2)]
        w2b = [c.sb(f"m_w2b{i}", [128, 128], BF16) for i in range(2)]
        dmj = c.sb("m_dmj", [128, DC * SLAB])
        b1c = c.sb("m_b1c", [128, NH])
        b2row = c.sb("m_b2r", [1, D])
        b2bc = c.sb("m_b2bc", [128, D])
        eot = c.sb("m_eot", [128, D])

        for ex in range(E):
            def tdr(e, inc, mk, ex=ex):
                for ch in range(DC):
                    inc(e.dma_start_transpose(
                        dfm[:, ch * SLAB:(ch + 1) * SLAB],
                        bass.AP(tensor=disp, offset=ex * SLAB * D + ch * 128,
                                ap=[[D, SLAB], [1, 128]])))
            c.step("sy", tdr, reads=[r_disp], writes=["m_dfm"], weight=16 * DC,
                   tag=f"me.t{ex}")
            emit_bias_cols(c, W["fc1_b"], H, b1c, "m_b1c", src_off=ex * H)
            c.step("sy", lambda e, inc, mk, ex=ex: inc(e.dma_start(
                out=b2row[0:1, :], in_=W["fc2_b"][None, ex * D:(ex + 1) * D])),
                writes=["m_b2r"], weight=16, tag=f"me.b2{ex}")

            def pb2(e, inc, mk):
                for hf in range(2):
                    mk(e.matmul(psU[:, 1024 + hf * 512:1024 + (hf + 1) * 512],
                                c.cst["ones128"][0:1, :],
                                b2row[0:1, hf * 512:(hf + 1) * 512],
                                start=True, stop=True))
            c.step("pe", pb2, reads=["m_b2r", "c_ones128"], writes=["psU"],
                   tag=f"me.bb{ex}")
            c.step("ve", lambda e, inc, mk: mk(e.tensor_copy(
                b2bc[:, :], psU[:, 1024:2048])),
                reads=["psU"], writes=["m_b2bc"], tag=f"me.bc{ex}")

            for hc in range(NH):
                pp = hc % 2
                c.step("sy", lambda e, inc, mk, ex=ex, hc=hc, pp=pp: inc(e.dma_start(
                    out=w1sl[pp][:, :].rearrange("p (ch j) -> p ch j", j=128),
                    in_=bass.AP(tensor=W["fc1_wt"].tensor,
                                offset=ex * D * H + hc * 128,
                                ap=[[H, 128], [128 * H, DC], [1, 128]]))),
                    writes=[f"m_w1sl{pp}"], weight=16, tag=f"me.w1.{ex}.{hc}")
                c.step("ve", lambda e, inc, mk, pp=pp: mk(e.tensor_copy(
                    w1b[pp][:, :], w1sl[pp][:, :])),
                    reads=[f"m_w1sl{pp}"], writes=[f"m_w1b{pp}"], tag="me.w1c")

                def pf1(e, inc, mk, hc=hc, pp=pp):
                    for seg, n in ((0, 512), (512, 128)):
                        for ch in range(DC):
                            mk(e.matmul(psA[:, seg:seg + n],
                                        w1b[pp][:, ch * 128:(ch + 1) * 128],
                                        dfm[:, ch * SLAB + seg:ch * SLAB + seg + n],
                                        start=(ch == 0), stop=(ch == DC - 1)))
                c.step("pe", pf1, reads=[f"m_w1b{pp}", "m_dfm"], writes=["psA"],
                       tag=f"me.f1.{ex}.{hc}")
                c.step("se", lambda e, inc, mk, hc=hc: mk(e.activation(
                    out=hbe[:, hc * SLAB:(hc + 1) * SLAB], in_=psA[:, 0:SLAB],
                    func=AF.Relu, bias=b1c[:, hc:hc + 1])),
                    reads=["psA", "m_b1c"], writes=["m_hbe"], tag=f"me.r.{ex}.{hc}")

            for dch in range(DC):
                pb = 0 if dch % 2 == 0 else 1024
                for hc in range(NH):
                    pp = 0
                    c.step("sy", lambda e, inc, mk, ex=ex, hc=hc, dch=dch, pp=pp:
                           inc(e.dma_start(
                               out=w2sl[pp][:, :],
                               in_=bass.AP(tensor=W["fc2_w"].tensor,
                                           offset=ex * H * D + hc * 128 * D + dch * 128,
                                           ap=[[D, 128], [1, 128]]))),
                           writes=[f"m_w2sl{pp}"], weight=16,
                           tag=f"me.w2.{ex}.{dch}.{hc}")
                    c.step("ve", lambda e, inc, mk, pp=pp: mk(e.tensor_copy(
                        w2b[pp][:, :], w2sl[pp][:, :])),
                        reads=[f"m_w2sl{pp}"], writes=[f"m_w2b{pp}"], tag="me.w2c")
                    def pf2(e, inc, mk, hc=hc, pp=pp, pb=pb):
                        for seg, n in ((0, 512), (512, 128)):
                            mk(e.matmul(psA[:, pb + seg:pb + seg + n],
                                        w2b[pp][:, :],
                                        hbe[:, hc * SLAB + seg:hc * SLAB + seg + n],
                                        start=(hc == 0), stop=(hc == NH - 1)))
                    c.step("pe", pf2, reads=[f"m_w2b{pp}", "m_hbe"],
                           writes=["psA"], tag=f"me.f2.{ex}.{dch}.{hc}")
                c.step("ve", lambda e, inc, mk, dch=dch, pb=pb: mk(e.tensor_copy(
                    dmj[:, dch * SLAB:(dch + 1) * SLAB], psA[:, pb:pb + SLAB])),
                    reads=["psA"], writes=["m_dmj"], tag=f"me.dv.{ex}.{dch}")

            for ct in range(5):
                def ptq(e, inc, mk, ct=ct):
                    for dch in range(DC):
                        mk(e.transpose(
                            psU[:, dch * 128:(dch + 1) * 128],
                            dmj[:, dch * SLAB + ct * 128:dch * SLAB + ct * 128 + 128],
                            c.cst["identf"][:, :]))
                c.step("pe", ptq, reads=["m_dmj", "c_identf"], writes=["psU"],
                       tag=f"me.tq.{ex}.{ct}")
                c.step("ve", lambda e, inc, mk: mk(e.tensor_add(
                    eot[:, :], psU[:, 0:D], b2bc[:, :])),
                    reads=["psU", "m_b2bc"], writes=["m_eot"], tag=f"me.eb{ex}")
                c.step("gp", lambda e, inc, mk, ex=ex, ct=ct: inc(e.dma_start(
                    out=eobuf.ap()[ex * SLAB + ct * 128:ex * SLAB + (ct + 1) * 128, :],
                    in_=eot[:, :])),
                    reads=["m_eot"], writes=[r_eo], weight=16, tag=f"me.eo{ex}.{ct}")

    if stage < 4:
        return pacc
    with scoped(c):
        pad0 = c.sb("m_pad0", [128, 512])
        oc = c.sb("m_oc2", [128, 8])
        oci = c.sb("m_oci2", [128, 2], I32)
        ga = c.sb("m_ga", [128, D])
        gb = c.sb("m_gb", [128, D])
        gsum = c.sb("m_gsum", [128, D])
        stok = c.sb("m_stok", [128, D])
        for tt in range(min(NT, MCT)):
            if sub == 0:
                c.step("gp", lambda e, inc, mk, tt=tt: inc(e.dma_start(
                    out=oc[:, :],
                    in_=coefbuf.ap().rearrange("p (g e) -> (p g) e", e=8)[
                        tt * 128:(tt + 1) * 128, :])),
                    reads=[r_coef], writes=["m_oc2"], weight=16, tag=f"mc.G{tt}")
            else:
                c.step("gp", lambda e, inc, mk, tt=tt: inc(e.indirect_dma_start(
                    oc[:, :], None, coefbuf.ap().rearrange("p (g e) -> (p g) e", e=8),
                    IndirectOffsetOnAxis(c.t_orows[:, tt:tt + 1], 0))),
                    reads=[r_coef, "orows"], writes=["m_oc2"], weight=16,
                    tag=f"mc.g{tt}")
            if sub >= 1.5:
                c.step("ve", lambda e, inc, mk: mk(e.tensor_copy(
                    oci[:, :], oc[:, 2:4])),
                    reads=["m_oc2"], writes=["m_oci2"], tag="mc.i")
            if sub < 2:
                continue
            c.step("ve", lambda e, inc, mk: mk(e.memset(ga[:, :], 0.0)),
                   writes=["m_ga"], tag="mc.za")
            c.step("ve", lambda e, inc, mk: mk(e.memset(gb[:, :], 0.0)),
                   writes=["m_gb"], tag="mc.zb")
            c.step("gp", lambda e, inc, mk: inc(e.indirect_dma_start(
                ga[:, :], None, eobuf.ap(),
                IndirectOffsetOnAxis(oci[:, 0:1], 0))),
                reads=[r_eo, "m_oci2"], writes=["m_ga"], weight=16, tag="mc.a")
            c.step("gp", lambda e, inc, mk: inc(e.indirect_dma_start(
                gb[:, :], None, eobuf.ap(),
                IndirectOffsetOnAxis(oci[:, 1:2], 0))),
                reads=[r_eo, "m_oci2"], writes=["m_gb"], weight=16, tag="mc.b")
            if sub < 3:
                continue
            c.step("ve", lambda e, inc, mk: mk(e.tensor_scalar(
                ga[:, :], ga[:, :], oc[:, 0:1], None, ALU.mult)),
                reads=["m_oc2"], writes=["m_ga"], tag="mc.ma")
            c.step("ve", lambda e, inc, mk: mk(e.scalar_tensor_tensor(
                gsum[:, :], gb[:, :], oc[:, 1:2], ga[:, :], ALU.mult, ALU.add)),
                reads=["m_gb", "m_oc2", "m_ga"], writes=["m_gsum"], tag="mc.s")
            c.step("se", lambda e, inc, mk: mk(e.activation(
                out=gsum[:, :], in_=gsum[:, :], func=AF.Silu)),
                writes=["m_gsum"], tag="mc.si")
            if sub < 4:
                continue
            def lds(e, inc, mk, tt=tt):
                for ch in range(DC):
                    inc(e.dma_start(
                        out=stok[:, ch * 128:(ch + 1) * 128],
                        in_=s_fm[ch * 128:(ch + 1) * 128, tt * 128:(tt + 1) * 128]))
            c.step("sy", lds, reads=[s_fm_res], writes=["m_stok"],
                   weight=16 * DC, tag=f"mc.l{tt}")

            def ptr2(e, inc, mk):
                for ch in range(DC):
                    mk(e.transpose(psA[:, ch * 128:(ch + 1) * 128],
                                   stok[:, ch * 128:(ch + 1) * 128],
                                   c.cst["identf"][:, :]))
            c.step("pe", ptr2, reads=["m_stok", "c_identf"], writes=["psA"],
                   tag="mc.t")
            c.step("ve", lambda e, inc, mk, tt=tt: mk(e.tensor_mul(
                pacc[:, tt * D:(tt + 1) * D], psA[:, 0:D], gsum[:, :])),
                reads=["psA", "m_gsum"], writes=["m_pacc"], tag="mc.p")
    return pacc


def build_layer0(nc, dbg=False, stage=99, sub=99):
    c = Ctx(nc)
    x_in = nc.dram_tensor("x_in", [T, D], F32, kind="ExternalInput").ap()
    x_out = nc.dram_tensor("x_out", [T, D], F32, kind="ExternalOutput").ap()
    wnames = {
        "wq_t": [D, D], "wk_t": [D, D], "wv_t": [D, D], "wo_t": [D, D],
        "bq": [D], "bk": [D], "bv": [D], "bo": [D],
        "relk_pad": [64, 4096],
        "gate_wt": [D, 8],
        "fc1_wt": [8, D, 4096], "fc2_w": [8, 4096, D],
        "fc1_b": [8 * 4096], "fc2_b": [8 * D],
        "eramp": [1, 8],
    }
    W = {k: nc.dram_tensor("w_" + k, shp, F32, kind="ExternalInput").ap()
         for k, shp in wnames.items()}
    selv = nc.dram_tensor("w_selv", [128, 2], F32, kind="ExternalInput").ap()
    orows = nc.dram_tensor("w_orows", [128, NT], I32, kind="ExternalInput").ap()
    s_fm = nc.dram_tensor("s_fm", [D, T], F32,
                          kind="ExternalOutput" if dbg else "Internal")
    declare_common(c)
    r_sfm = c.newres("r_sfm")
    r_xout = c.newres("r_xout")
    c.t_selv = c.sb("selv", [128, 2])
    c.t_orows = c.sb("orows", [128, NT], I32)
    c.t_eramp = c.sb("eramp", [1, 8])

    def ldmisc(e, inc, mk):
        inc(e.dma_start(out=c.t_selv[:, :], in_=selv[:, :]))
        inc(e.dma_start(out=c.t_orows[:, :], in_=orows[:, :]))
        inc(e.dma_start(out=c.t_eramp[:, :], in_=W["eramp"][:, :]))
    c.step("sy", ldmisc, writes=["selv", "orows", "eramp"], weight=48, tag="l0misc")

    with scoped(c):
        xnfm = c.sb("l_xnfm", [128, DC * T], BF16)
        emit_ln_pass(c, x_in, xnfm, "l_xnfm")
        emit_attn_l3(c, W, xnfm, s_fm.ap(), r_sfm, pfx="a0", relv=False,
                     bscale=0.125)

    pacc_t = emit_moe_ffn(c, W, x_in, s_fm.ap(), r_sfm, stage=stage, dbg=dbg,
                          sub=sub)
    emit_residual_combine(c, x_in, x_out, r_xout, pacc_t, "m_pacc")
    c.step("gp", lambda e, inc, mk: None, reads=[r_xout], tag="final")
    return c


def _wmap_l0(inputs):
    CA = np.ascontiguousarray
    relk_idx = np.clip(np.arange(4096) - 1983, 0, 72)
    return {
        "w_wq_t": CA(inputs["L0_attn_Wq"].T), "w_wk_t": CA(inputs["L0_attn_Wk"].T),
        "w_wv_t": CA(inputs["L0_attn_Wv"].T), "w_wo_t": CA(inputs["L0_attn_Wo"].T),
        "w_bq": inputs["L0_attn_bq"], "w_bk": inputs["L0_attn_bk"],
        "w_bv": inputs["L0_attn_bv"], "w_bo": inputs["L0_attn_bo"],
        "w_relk_pad": CA(inputs["L0_attn_dist_emb"][relk_idx].T),
        "w_gate_wt": CA(inputs["L0_ffn_gate_w"].T),
        "w_fc1_wt": CA(inputs["L0_ffn_fc1_w"].transpose(0, 2, 1)),
        "w_fc2_w": CA(inputs["L0_ffn_fc2_w"]),
        "w_fc1_b": CA(inputs["L0_ffn_fc1_b"].reshape(-1)),
        "w_fc2_b": CA(inputs["L0_ffn_fc2_b"].reshape(-1)),
        "w_eramp": (np.arange(8) * 640).astype(np.float32).reshape(1, 8),
    }


def _wmap_l1(inputs):
    CA = np.ascontiguousarray
    return {
        "w_W_qkv": inputs["L1_attn_W_qkv"], "w_b_qkv": inputs["L1_attn_b_qkv"],
        "w_W_out": inputs["L1_attn_W_out"], "w_b_out": inputs["L1_attn_b_out"],
        "w_clip_x": inputs["L1_ffn_clip_x"], "w_clip_down": inputs["L1_ffn_clip_down"],
        "w_w_gate": inputs["L1_ffn_w_gate"], "w_w_up": inputs["L1_ffn_w_up"],
        "w_w_down": inputs["L1_ffn_w_down"],
        "w_w_gate_t": CA(inputs["L1_ffn_w_gate"].T),
        "w_w_up_t": CA(inputs["L1_ffn_w_up"].T),
        "w_w_down_t": CA(inputs["L1_ffn_w_down"].T),
    }


def _wmap_l2(inputs):
    CA = np.ascontiguousarray
    return {
        "w_wq_t": CA(inputs["L2_attn_Wq"].T), "w_wk_t": CA(inputs["L2_attn_Wk"].T),
        "w_wv_t": CA(inputs["L2_attn_Wv"].T), "w_wo_t": CA(inputs["L2_attn_Wo"].T),
        "w_bo": inputs["L2_attn_bo"],
        "w_proj_t": CA(np.transpose(inputs["L2_attn_proj"], (0, 2, 1))),
        "w_w1_t": CA(inputs["L2_ffn_w1"].T), "w_b1": inputs["L2_ffn_b1"],
        "w_w2_t": CA(inputs["L2_ffn_w2"].T), "w_b2": inputs["L2_ffn_b2"],
    }


def _wmap_l3(inputs):
    CA = np.ascontiguousarray
    _r = np.arange(-1024, 3072)
    ik = np.where(_r < 0, 2047 + _r, np.minimum(_r, 2046))
    _r2 = np.arange(4096) - 1025
    iv = np.where(_r2 < 0, 2047 + _r2, np.minimum(_r2, 2046))
    return {
        "w_wq_t": CA(inputs["L3_attn_Wq"].T), "w_wk_t": CA(inputs["L3_attn_Wk"].T),
        "w_wv_t": CA(inputs["L3_attn_Wv"].T), "w_wo_t": CA(inputs["L3_attn_Wo"].T),
        "w_bq": inputs["L3_attn_bq"], "w_bk": inputs["L3_attn_bk"],
        "w_bv": inputs["L3_attn_bv"], "w_bo": inputs["L3_attn_bo"],
        "w_relk_pad": CA(inputs["L3_attn_relative_k"][ik].T),
        "w_relv_pad": CA(inputs["L3_attn_relative_v"][iv]),
        "w_clip_x": inputs["L3_ffn_clip_x"], "w_clip_down": inputs["L3_ffn_clip_down"],
        "w_w_gate": inputs["L3_ffn_w_gate"], "w_w_up": inputs["L3_ffn_w_up"],
        "w_w_down": inputs["L3_ffn_w_down"],
        "w_w_gate_t": CA(inputs["L3_ffn_w_gate"].T),
        "w_w_up_t": CA(inputs["L3_ffn_w_up"].T),
        "w_w_down_t": CA(inputs["L3_ffn_w_down"].T),
    }


def kernel(**inputs):
    CA = np.ascontiguousarray
    consts = host_constants()
    x = np.asarray(inputs["L0_attn_hidden_states"], np.float32)
    layers = [
        (build_layer0, _wmap_l0),
        (build_layer1, _wmap_l1),
        (build_layer2, _wmap_l2),
        (build_layer3, _wmap_l3),
    ]
    for li, (builder, wfn) in enumerate(layers):
        nc = bass.Bass("TRN2", target_bir_lowering=False, debug=False)
        c = builder(nc)
        c.finish()
        wmap = wfn(inputs)
        in_maps = []
        for b in range(NCORES):
            m = {"x_in": CA(x[b])}
            m.update(wmap)
            if li == 0:
                sv = np.zeros((128, 2), np.float32)
                sv[16 * b, 0] = 1.0
                sv[16 * b:16 * b + 16, 1] = 1.0
                m["w_selv"] = sv
                m["w_orows"] = (b * 2048 + np.arange(2048).reshape(16, 128).T
                                ).astype(np.int32).copy()
            for k, v in consts.items():
                m["cst_" + k] = v
            in_maps.append(m)
        res = run_bass_kernel_spmd(nc, in_maps, core_ids=list(range(NCORES)))
        x = np.stack([res.results[b]["x_out"] for b in range(NCORES)])
    return x.astype(np.float32)
